# version 30 (final)
# baseline (speedup 1.0000x reference)
import numpy as np
import ml_dtypes
import concourse.bass as bass
import concourse.mybir as mybir
from concourse.bass_utils import run_bass_kernel_spmd
from contextlib import ExitStack

F32 = mybir.dt.float32
BF16 = mybir.dt.bfloat16
AF = mybir.ActivationFunctionType
ALU = mybir.AluOpType
AX = mybir.AxisListType

NCORES = 8
NB = 2
T = 2048
C = 1024
NT = 16
NCT = 8
NH = 16
NE = 8
HID = 4096
NHT = 32
EPS = 1e-5
NEGC = -30000.0

ENGINES = ("tensor", "vector", "scalar", "sync", "gpsimd")


class Prog:
    def __init__(self):
        self.ops = {e: [] for e in ENGINES}
        self.counts = {}
        self.semnames = []

    def sem(self, name):
        self.semnames.append(name)
        self.counts[name] = 0

    SELF_WAIT = True

    def emit(self, eng, fn, waits=(), inc=None, n=1):
        waits = [w for w in waits
                 if w is not None and w[1] is not None and w[1] > 0]
        if eng == "vector":
            if self.SELF_WAIT and self.counts["dve"] > 0:
                waits.append(("dve", self.counts["dve"]))
            inc, n = "dve", 1
        elif eng == "scalar":
            if self.SELF_WAIT and self.counts["act"] > 0:
                waits.append(("act", self.counts["act"]))
            inc, n = "act", 1
        tok = None
        if inc is not None:
            self.counts[inc] += n
            tok = self.counts[inc]
        self.ops[eng].append((fn, tuple(waits), inc, n))
        return tok

    def mm(self, out, lhsT, rhs, start, stop, waits=(), inc=None):
        out = out if isinstance(out, bass.AP) else out[:]
        lhsT = lhsT if isinstance(lhsT, bass.AP) else lhsT[:]
        rhs = rhs if isinstance(rhs, bass.AP) else rhs[:]

        def fn(t, out=out, lhsT=lhsT, rhs=rhs, start=start, stop=stop):
            return t.matmul(out, lhsT, rhs, start=start, stop=stop,
                            skip_group_check=True)
        return self.emit("tensor", fn, waits, inc=inc)

    def dma(self, queue, dst, src, sem, waits=()):
        def fn(e, dst=dst, src=src):
            return e.dma_start(dst, src)
        return self.emit(queue, fn, waits, inc=sem, n=16)


class _Stop(Exception):
    pass


REGS = {}


def build(debug_taps=False, stage=99, fvar=0):
    nc = bass.Bass("TRN2", target_bir_lowering=False)
    p = Prog()

    def ck(k):
        if stage <= k:
            raise _Stop()

    x_d = nc.dram_tensor("x", [NB, T, C], F32, kind="ExternalInput")
    wdq_d = nc.dram_tensor("wdq", [128, C], F32, kind="ExternalInput")
    wuq_d = nc.dram_tensor("wuq", [C, 128], F32, kind="ExternalInput")
    wdkv_d = nc.dram_tensor("wdkv", [128, C], F32, kind="ExternalInput")
    wuk_d = nc.dram_tensor("wuk", [C, 128], F32, kind="ExternalInput")
    wuv_d = nc.dram_tensor("wuv", [C, 128], F32, kind="ExternalInput")
    wo_d = nc.dram_tensor("wo", [C, C], F32, kind="ExternalInput")
    wr_d = nc.dram_tensor("wr", [C, NE], F32, kind="ExternalInput")
    w1_d = nc.dram_tensor("w1", [NE * 8, 128, 4096], F32, kind="ExternalInput")
    w2_d = nc.dram_tensor("w2", [NE * 8, 128, 4096], F32, kind="ExternalInput")
    identf_d = nc.dram_tensor("identf", [128, 128], F32, kind="ExternalInput")
    identb_d = nc.dram_tensor("identb", [128, 128], BF16, kind="ExternalInput")
    ones1f_d = nc.dram_tensor("ones1f", [1, 128], F32, kind="ExternalInput")
    ones1b_d = nc.dram_tensor("ones1b", [1, 128], BF16, kind="ExternalInput")
    onescol_d = nc.dram_tensor("onescol", [128, 1], BF16, kind="ExternalInput")
    mask2_d = nc.dram_tensor("mask2", [128, 128], F32, kind="ExternalInput")
    ones8_d = nc.dram_tensor("ones8", [8, 1], F32, kind="ExternalInput")
    ecol_d = nc.dram_tensor("ecol", [8, 1], F32, kind="ExternalInput")
    iota_d = nc.dram_tensor("iota_tok", [128, NT], mybir.dt.int32,
                            kind="ExternalInput")
    out_d = nc.dram_tensor("out", [NB, T, C], F32, kind="ExternalOutput")
    rspill_d = nc.dram_tensor("rspill", [128, NT * C], F32)
    w1bf_d = nc.dram_tensor("w1bf", [NE * 8, 128, 4096], BF16)
    w2bf_d = nc.dram_tensor("w2bf", [NE * 8, 128, 4096], BF16)
    xtok_d = nc.dram_tensor("xtok", [T, C], BF16)
    ytok_d = nc.dram_tensor("ytok", [T, C], BF16)
    idxl_d = nc.dram_tensor("idxl", [128, 32], mybir.dt.int32)

    taps = {}
    if debug_taps:
        for name, shape, dt in (
            ("tap_xmid", [128, NT * C], F32),
            ("tap_gate", [128, NT * NE], F32),
            ("tap_ckvT", [128, T], BF16),
            ("tap_ckvN", [128, T], BF16),
            ("tap_t1T", [128, T], BF16),
            ("tap_xlnT", [128, NCT * T], BF16),
            ("tap_p", [128, NT * T], BF16),
            ("tap_Psb", [128, NH * 128], BF16),
            ("tap_idx", [128, 32], mybir.dt.int32),
            ("tap_gxT", [128, NCT * 512], BF16),
            ("tap_ybf", [128, NCT * 512], BF16),
            ("tap_hg", [128, NHT * 512], BF16),
            ("tap_Msb", [128, C], BF16),
        ):
            taps[name] = nc.dram_tensor(name, shape, dt, kind="ExternalOutput")

    for s in ("dc", "dcg", "dcw", "dx", "dsp", "drl0", "drl1", "dout0", "dout1", "dtap",
              "w10", "w11", "w20", "w21", "pe", "act", "dve",
              "dxt", "didx", "dgx0", "dgx1", "dsc0", "dsc1", "dyl", "dz0"):
        p.sem(s)

    es = ExitStack()
    sbuf = lambda name, shape, dt: es.enter_context(
        nc.sbuf_tensor(name + "_s", shape, dt))

    identf = sbuf("identf", [128, 128], F32)
    identb = sbuf("identb", [128, 128], BF16)
    ones1f = sbuf("ones1f", [1, 128], F32)
    ones1b = sbuf("ones1b", [1, 128], BF16)
    onescol = sbuf("onescol", [128, 1], BF16)
    mask2 = sbuf("mask2", [128, 128], F32)
    wdq = sbuf("wdq", [128, C], BF16)
    wdkv = sbuf("wdkv", [128, C], BF16)
    wr = sbuf("wr", [128, NCT * NE], BF16)
    wdqT = sbuf("wdqT", [128, NCT * 128], BF16)
    wdkvT = sbuf("wdkvT", [128, NCT * 128], BF16)
    P_sb = sbuf("P_sb", [128, NH * 128], BF16)
    M_sb = sbuf("M_sb", [128, C], BF16)
    xlnT = sbuf("xlnT", [128, NCT * T], BF16)
    gate = sbuf("gate", [128, NT * NE], F32)
    lnc = sbuf("lnc", [128, 8], F32)
    gmask = sbuf("gmask", [128, NE], F32)
    gm = sbuf("gm", [128, NE], F32)
    gc = sbuf("gc", [128, 8], F32)
    gsc = sbuf("gsc", [128, NE], F32)
    mch = sbuf("mch", [128, 8], F32)
    sch = sbuf("sch", [128, 8], F32)
    ones8 = sbuf("ones8", [8, 1], F32)
    ecol = sbuf("ecol", [8, 1], F32)
    iota_sb = sbuf("iota_sb", [128, NT], mybir.dt.int32)
    posf = sbuf("posf", [128, 16], F32)
    posi = sbuf("posi", [128, 16], mybir.dt.int32)
    idxini = sbuf("idxini", [128, 32], mybir.dt.int32)
    idx_sb = sbuf("idx_sb", [128, 32], mybir.dt.int32)
    wcol = sbuf("wcol", [128, NT], F32)
    mnegf = sbuf("mnegf", [128, NH], F32)
    bcol = sbuf("bcol", [128, 1], F32)
    sumsf = sbuf("sumsf", [128, 2 * NH], F32)
    rinv16 = sbuf("rinv16", [128, NH], F32)

    AR = 158 * 1024
    arena = sbuf("arena", [128, AR], mybir.dt.uint8)
    base = nc.lookup_mloc(arena).addr
    _cur = [0]

    def at(name, shape, dt, off=None):
        nbytes = int(np.prod(shape[1:])) * mybir.dt.size(dt)
        if off is None:
            off = _cur[0]
        off = (off + 31) // 32 * 32
        assert off + nbytes <= AR, (name, off, nbytes)
        h = nc.alloc_sbuf_tensor_at(name, shape, dt, offset=base + off)
        _cur[0] = off + nbytes
        return h

    h_g = at("h_g", [128, NHT * 512], BF16, 0)
    gxT = at("gxT", [128, 2 * NCT * 512], BF16)
    gx_tok = at("gx_tok", [128, 2 * 1024], BF16)
    ybf = at("ybf", [128, NCT * 512], BF16)
    ytt = at("ytt", [128, 2 * 1024], BF16)
    w1ring = at("w1ring", [128, 2 * 4096], BF16)
    w2ring = at("w2ring", [128, 2 * 4096], BF16)
    gateT_sb = at("gateT_sb", [8, T], F32)
    maskr = at("maskr", [8, T], F32)
    scanr = at("scanr", [8, T], F32)
    rowa = at("rowa", [8, T], F32)
    rowb = at("rowb", [8, T], F32)
    yld = at("yld", [128, 2 * 1024], BF16)
    posrow = gateT_sb
    out_st = at("out_st", [128, 2 * 1024], F32)
    rl_buf = at("rl_buf", [128, 2 * 1024], F32)

    _cur[0] = 0
    R = at("R", [128, NT * C], F32, 0)
    p_ar = at("p_ar", [128, NT * T], BF16)
    ql_sb = at("ql_sb", [128, T], BF16)
    t1T = at("t1T", [128, T], BF16)
    ckvT = at("ckvT", [128, T], BF16)
    ckvN = at("ckvN", [128, T], BF16)
    ctx_sb = at("ctx_sb", [128, T], BF16)
    sq_sb = at("sq_sb", [128, C], F32)
    xscr = at("xscr", [128, C], BF16)
    ptk = at("ptk", [128, T], BF16)

    _cur[0] = 0
    wo_sb = at("wo_sb", [128, NCT * C], BF16, 0)
    woT_sb = at("woT_sb", [128, NCT * C], BF16)
    a1_sb = at("a1_sb", [128, 128], BF16)
    k_sb = at("k_sb", [128, NCT * 128], BF16)
    wuq = at("wuq", [128, NCT * 128], BF16)
    wuk = at("wuk", [128, NCT * 128], BF16)
    wuv = at("wuv", [128, NCT * 128], BF16)


    ps_big = es.enter_context(nc.psum_tensor("ps_big", [128, 2048], F32))
    ps_a = es.enter_context(nc.psum_tensor("ps_a", [128, 512], F32))
    ps_b = es.enter_context(nc.psum_tensor("ps_b", [128, 512], F32))
    ps_c = es.enter_context(nc.psum_tensor("ps_c", [128, 512], F32))
    ps_d = es.enter_context(nc.psum_tensor("ps_d", [128, 512], F32))

    try:
        p.dma("sync", identf[:], identf_d.ap(), "dc")
        p.dma("sync", identb[:], identb_d.ap(), "dc")
        p.dma("sync", ones1f[:], ones1f_d.ap(), "dc")
        p.dma("sync", ones1b[:], ones1b_d.ap(), "dc")
        p.dma("sync", onescol[:], onescol_d.ap(), "dc")
        p.dma("sync", mask2[:], mask2_d.ap(), "dc")
        p.dma("gpsimd", wdq[:], wdq_d.ap(), "dcg")
        p.dma("gpsimd", wdkv[:], wdkv_d.ap(), "dcg")
        for d, dst in ((wuq_d, wuq), (wuk_d, wuk), (wuv_d, wuv)):
            p.dma("gpsimd", dst[:],
                  bass.AP(d, 0, [[128, 128], [128 * 128, NCT], [1, 128]]),
                  "dcg")
        p.dma("gpsimd", wr[:],
              bass.AP(wr_d, 0, [[NE, 128], [128 * NE, NCT], [1, NE]]), "dcg")
        p.dma("sync", ones8[:], ones8_d.ap(), "dc")
        p.dma("sync", ecol[:], ecol_d.ap(), "dc")
        p.dma("sync", iota_sb[:], iota_d.ap(), "dc")
        p.dma("gpsimd", wo_sb[:],
              bass.AP(wo_d, 0, [[C, 128], [128 * C, NCT], [1, C]]), "dcg")
        DC = ("dc", p.counts["dc"])
        DCG = ("dcg", p.counts["dcg"])
        bcol_tok = p.emit("vector", lambda v: v.memset(bcol[:], -12.0))
        bexp0 = [True]
        ck(0)

        tok = None
        for jt in range(NCT):
            tok = p.mm(ps_big[:, 0:128], wuq[:, jt * 128:(jt + 1) * 128],
                       wuk[:, jt * 128:(jt + 1) * 128], jt == 0, jt == NCT - 1,
                       waits=(DC, DCG) if jt == 0 else (),
                       inc="pe" if jt == NCT - 1 else None)
        a1_tok = p.emit("scalar", lambda s: s.copy(a1_sb[:], ps_big[:, 0:128]),
                        waits=(("pe", tok),))
        ck(0.2)
        for ct in range(NCT):
            tok = p.mm(ps_big[:, 512 + ct * 128:512 + (ct + 1) * 128],
                       wdq[:, ct * 128:(ct + 1) * 128], a1_sb[:], True, True,
                       waits=(("act", a1_tok),) if ct == 0 else (), inc="pe")
        k_tok = p.emit("scalar", lambda s: s.copy(k_sb[:], ps_big[:, 512:1536]),
                       waits=(("pe", tok),))
        ck(0.3)
        last_evac = k_tok
        ph_order = [h for h in range(NH) if h % 2 == 0] + [None] + \
                   [h for h in range(NH) if h % 2 == 1]
        k = 0
        for h in ph_order:
            if h is None:
                p.mm(ps_big[:, 0:128], wuq[:, 0:128], wuk[:, 0:128],
                     True, True)
                continue
            jt, off = h // 2, (h % 2) * 64
            tok = p.mm(ps_big[:, 1536 + (k % 4) * 128:1536 + (k % 4 + 1) * 128],
                       wuq[off:off + 64, jt * 128:(jt + 1) * 128],
                       k_sb[off:off + 64, jt * 128:(jt + 1) * 128], True, True,
                       waits=(("act", last_evac),) if k % 4 == 0 else (),
                       inc="pe")
            if k % 4 == 3:
                grp = ([h2 for h2 in ph_order if h2 is not None]
                       [k - 3:k + 1])
                for j, hh in enumerate(grp):
                    last_evac = p.emit("scalar", lambda s, hh=hh, j=j:
                                       s.activation(
                                           P_sb[:, hh * 128:(hh + 1) * 128],
                                           ps_big[:, 1536 + j * 128:
                                                  1536 + (j + 1) * 128],
                                           AF.Copy, scale=0.125),
                                       waits=(("pe", tok),) if j == 0 else ())
            k += 1
        ck(0.4)
        blocks = [(ct, jt) for ct in range(NCT) for jt in range(NCT)]
        evac_hist = [last_evac, last_evac]
        for gi in range(0, 64, 4):
            reg = (gi // 4) % 2
            c0 = reg * 512
            toks = []
            for k, (ct, jt) in enumerate(blocks[gi:gi + 4]):
                toks.append(p.mm(
                    ps_big[:, c0 + k * 128:c0 + (k + 1) * 128],
                    wo_sb[:, ct * C + jt * 128:ct * C + (jt + 1) * 128],
                    identb[:], True, True,
                    waits=(("act", evac_hist[reg]),) if k == 0 else (), inc="pe"))
            for k, (ct, jt) in enumerate(blocks[gi:gi + 4]):
                e = p.emit("scalar", lambda s, ct=ct, jt=jt, c0=c0, k=k: s.copy(
                    woT_sb[:, jt * C + ct * 128:jt * C + (ct + 1) * 128],
                    ps_big[:, c0 + k * 128:c0 + (k + 1) * 128]),
                    waits=(("pe", toks[k]),))
            evac_hist[reg] = e
        for jt in range(NCT):
            w0 = (("act", evac_hist[0]), ("act", evac_hist[1])) if jt == 0 else ()
            p.mm(ps_big[:, 0:512], wuv[:, jt * 128:(jt + 1) * 128],
                 woT_sb[:, jt * C:jt * C + 512], jt == 0, jt == NCT - 1, waits=w0)
            tok = p.mm(ps_big[:, 512:1024], wuv[:, jt * 128:(jt + 1) * 128],
                       woT_sb[:, jt * C + 512:jt * C + 1024], jt == 0, jt == NCT - 1,
                       inc="pe" if jt == NCT - 1 else None)
        ck(0.5)
        m_tok = p.emit("scalar", lambda s: s.copy(M_sb[:], ps_big[:, 0:1024]),
                       waits=(("pe", tok),))
        ck(0.6)
        evac_hist = [m_tok, m_tok]
        for i, (src, dst) in enumerate(
                [(wdq, wdqT)] * 0 + [(wdq, wdqT), (wdkv, wdkvT)]):
            for ct in range(NCT):
                reg = ct % 2
                c0 = 1024 + reg * 512
                tok = p.mm(ps_big[:, c0:c0 + 128],
                           src[:, ct * 128:(ct + 1) * 128], identb[:], True, True,
                           waits=(("act", evac_hist[reg]),), inc="pe")
                evac_hist[reg] = p.emit("scalar", lambda s, dst=dst, ct=ct, c0=c0:
                                        s.copy(dst[:, ct * 128:(ct + 1) * 128],
                                               ps_big[:, c0:c0 + 128]),
                                        waits=(("pe", tok),))
        if debug_taps:
            p.dma("sync", taps["tap_Psb"].ap(), P_sb[:], "dtap",
                  waits=(("act", p.counts["act"]),))
            p.dma("sync", taps["tap_Msb"].ap(), M_sb[:], "dtap",
                  waits=(("act", p.counts["act"]),))

        setup_end = dict(pe=p.counts["pe"], act=p.counts["act"], dve=p.counts["dve"])
        ck(1)
        prev_end = setup_end
        prev_dout = (0, 0)
        w1_use = [0, 0]
        w2_use = [0, 0]
        rl_use = [0, 0]
        out_use = [0, 0]
        gx_uses = [0, 0]
        sc_uses = [0, 0]

        ln_state = {"prev_ev": None, "xspill": None}

        def ln_tile(tt, extra_first=(), region="hi", trans_wait=None):
            sl = R[:, tt * C:(tt + 1) * C]
            p.emit("vector", lambda v, sl=sl: v.tensor_mul(sq_sb[:], sl, sl),
                   waits=extra_first)
            p.emit("vector", lambda v, sl=sl: v.reduce_sum(lnc[:, 0:1], sl, AX.X))
            p.emit("vector", lambda v: v.reduce_sum(lnc[:, 1:2], sq_sb[:], AX.X))
            p.emit("vector", lambda v: v.tensor_scalar_mul(lnc[:, 2:3], lnc[:, 0:1],
                                                           1.0 / C))
            p.emit("vector", lambda v: v.tensor_scalar(
                lnc[:, 3:4], lnc[:, 2:3], lnc[:, 2:3], -EPS, ALU.mult, ALU.add))
            d5 = p.emit("vector", lambda v: v.scalar_tensor_tensor(
                lnc[:, 4:5], lnc[:, 1:2], 1.0 / C, lnc[:, 3:4], ALU.mult,
                ALU.subtract))
            p.emit("scalar", lambda s: s.activation(lnc[:, 5:6], lnc[:, 4:5],
                                                    AF.Ln),
                   waits=(("dve", d5),))
            a2 = p.emit("scalar", lambda s: s.activation(lnc[:, 6:7], lnc[:, 5:6],
                                                         AF.Exp, scale=-0.5))
            p.emit("vector", lambda v: v.tensor_scalar_mul(
                lnc[:, 7:8], lnc[:, 2:3], -1.0))
            v3 = p.emit("vector", lambda v, sl=sl: v.tensor_scalar(
                xscr[:], sl, lnc[:, 7:8], lnc[:, 6:7], ALU.add, ALU.mult),
                waits=(("act", a2), ln_state.get("xspill")))
            if region == "hi":
                toks = []
                for ct in range(NCT):
                    w0 = ()
                    if ct == 0:
                        w0 = [("dve", v3)]
                        if ln_state["prev_ev"] is not None:
                            w0.append(("act", ln_state["prev_ev"]))
                    toks.append(p.mm(
                        ps_big[:, 1024 + ct * 128:1024 + (ct + 1) * 128],
                        xscr[:, ct * 128:(ct + 1) * 128], identb[:],
                        True, True, waits=w0,
                        inc="pe" if ct == NCT - 1 else None))
                ev = None
                for ct in range(NCT):
                    ev = p.emit("scalar", lambda s, ct=ct, tt=tt: s.copy(
                        xlnT[:, ct * T + tt * 128:ct * T + (tt + 1) * 128],
                        ps_big[:, 1024 + ct * 128:1024 + (ct + 1) * 128]),
                        waits=(("pe", toks[-1]),) if ct == 0 else ())
                ln_state["prev_ev"] = ev
                return ev
            p.dma("sync", bass.AP(xtok_d, tt * 128 * C, [[C, 128], [1, C]]),
                  xscr[:], "dxt", waits=(("dve", v3),))
            ln_state["xspill"] = ("dxt", p.counts["dxt"])
            ev_ring = [None, None]
            ev = None
            for ct in range(NCT):
                psr = ps_c if ct % 2 == 0 else ps_d
                w0 = [("dve", v3)] if ct == 0 else []
                if ct < 2:
                    w0.append(("dve", trans_wait))
                elif ev_ring[ct % 2] is not None:
                    w0.append(("act", ev_ring[ct % 2]))
                tm = p.mm(psr[:, 0:128],
                          xscr[:, ct * 128:(ct + 1) * 128], identb[:],
                          True, True, waits=w0, inc="pe")
                ev = p.emit("scalar", lambda s, ct=ct, tt=tt, psr=psr: s.copy(
                    xlnT[:, ct * T + tt * 128:ct * T + (tt + 1) * 128],
                    psr[:, 0:128]), waits=(("pe", tm),))
                ev_ring[ct % 2] = ev
            return ev

        for b in range(NB):
            bw = (("pe", prev_end["pe"]), ("act", prev_end["act"]),
                  ("dve", prev_end["dve"]), ("dout0", prev_dout[0]),
                  ("dout1", prev_dout[1]))
            p.dma("sync", R[:], bass.AP(x_d, b * T * C,
                                        [[C, 128], [128 * C, NT], [1, C]]),
                  "dx", waits=bw)
            DX = ("dx", p.counts["dx"])
            if b == 0:
                first_cast = True
                for wsrc, wdst in ((w1_d, w1bf_d), (w2_d, w2bf_d)):
                    for i in range(NE):
                        cw = (DC, DCG, DX) if first_cast else ()
                        first_cast = False
                        p.dma("gpsimd",
                              bass.AP(wdst, i * 8 * 128 * 4096,
                                      [[4096, 8 * 128], [1, 4096]]),
                              bass.AP(wsrc, i * 8 * 128 * 4096,
                                      [[4096, 8 * 128], [1, 4096]]),
                              "dcw", waits=cw)
                DCW = ("dcw", p.counts["dcw"])

            pdw = (("dout0", prev_dout[0]), ("dout1", prev_dout[1]))
            first_v = [DX, ("pe", prev_end["pe"]),
                       ("act", prev_end["act"])] + list(pdw)
            first_s = [DX, ("pe", prev_end["pe"]),
                       ("dve", prev_end["dve"])] + list(pdw)
            first_t = [DX, ("dve", prev_end["dve"]),
                       ("act", prev_end["act"])] + list(pdw)

            if b > 0:
                ln_state["prev_ev"] = prev_end["act"]
            ln_tok = None
            for tt in range(NT):
                extra = tuple(first_v) if tt == 0 else ()
                ln_tok = ln_tile(tt, extra_first=extra)

            if b == 0:
                ck(2)
            for ch in range(4):
                for ct in range(NCT):
                    w0 = []
                    if ch == 0 and ct == 0:
                        w0 = [("act", ln_tok)] + first_t
                    tok = p.mm(ps_big[:, ch * 512:(ch + 1) * 512],
                               wdkvT[:, ct * 128:(ct + 1) * 128],
                               xlnT[:, ct * T + ch * 512:ct * T + (ch + 1) * 512],
                               ct == 0, ct == NCT - 1, waits=w0,
                               inc="pe" if ct == NCT - 1 else None)
                p.emit("scalar", lambda s, ch=ch: s.copy(
                    ckvT[:, ch * 512:(ch + 1) * 512],
                    ps_big[:, ch * 512:(ch + 1) * 512]), waits=(("pe", tok),))
            ckvT_tok = p.counts["act"]
            for st in range(NT):
                for ct in range(NCT):
                    w0 = (("act", ckvT_tok),) if (st == 0 and ct == 0) else ()
                    tok = p.mm(ps_big[:, st * 128:(st + 1) * 128],
                               xlnT[:, ct * T + st * 128:ct * T + (st + 1) * 128],
                               wdkvT[:, ct * 128:(ct + 1) * 128],
                               ct == 0, ct == NCT - 1, waits=w0,
                               inc="pe" if ct == NCT - 1 else None)
            for ch in range(4):
                p.emit("scalar", lambda s, ch=ch: s.copy(
                    ckvN[:, ch * 512:(ch + 1) * 512],
                    ps_big[:, ch * 512:(ch + 1) * 512]),
                    waits=(("pe", tok),) if ch == 0 else ())
            ckvN_tok = p.counts["act"]
            for ch in range(4):
                for ct in range(NCT):
                    w0 = (("act", ckvN_tok),) if (ch == 0 and ct == 0) else ()
                    tok = p.mm(ps_big[:, ch * 512:(ch + 1) * 512],
                               wdqT[:, ct * 128:(ct + 1) * 128],
                               xlnT[:, ct * T + ch * 512:ct * T + (ch + 1) * 512],
                               ct == 0, ct == NCT - 1, waits=w0,
                               inc="pe" if ct == NCT - 1 else None)
                p.emit("scalar", lambda s, ch=ch: s.copy(
                    t1T[:, ch * 512:(ch + 1) * 512],
                    ps_big[:, ch * 512:(ch + 1) * 512]), waits=(("pe", tok),))
            t1_tok = p.counts["act"]
            if b == 0:
                ck(3)

            NSLOT = 4
            a_ring = [("act", t1_tok)] * NSLOT
            a_seq = [0]
            t_seq = [0]

            def slot_ap(slot, lo, hi):
                if slot < 4:
                    return ps_big[:, slot * 512 + lo:slot * 512 + hi]
                return (ps_a if slot == 4 else ps_b)[:, lo:hi]

            def emit_ql_group(qt, g):
                slot = a_seq[0] % NSLOT
                toks = []
                for k in range(4):
                    h = g * 4 + k
                    w0 = [a_ring[slot]] if k == 0 else []
                    toks.append(p.mm(
                        slot_ap(slot, k * 128, (k + 1) * 128),
                        P_sb[:, h * 128:(h + 1) * 128],
                        t1T[:, qt * 128:(qt + 1) * 128], True, True,
                        waits=w0, inc="pe" if k == 3 else None))
                ev = p.emit("scalar", lambda s, g=g, slot=slot: s.copy(
                    ql_sb[:, g * 512:(g + 1) * 512], slot_ap(slot, 0, 512)),
                    waits=(("pe", toks[-1]),))
                a_ring[slot] = ("act", ev)
                a_seq[0] += 1
                return ev

            def emit_tr_group(qt_p, h_p, g4, exps_p, guard_p):
                slot = a_seq[0] % NSLOT
                st0 = g4 * 4
                nst = min(4, qt_p + 1 - st0)
                tm = None
                for k in range(nst):
                    st = st0 + k
                    w0 = []
                    if k == 0:
                        w0 = [a_ring[slot],
                              ("act", exps_p[min(g4, len(exps_p) - 1)])]
                    tm = p.mm(slot_ap(slot, k * 128, (k + 1) * 128),
                              ptk[:, st * 128:(st + 1) * 128], identb[:],
                              True, True, waits=w0,
                              inc="pe" if k == nst - 1 else None)
                eng = "vector"
                ew = [("pe", tm)]
                if guard_p is not None:
                    ew.append(guard_p)
                dap = bass.AP(p_ar, st0 * T + h_p * 128,
                              [[NT * T, 128], [T, nst], [1, 128]])
                if eng == "vector":
                    evt = p.emit(eng, lambda v, dap=dap, slot=slot,
                                 nst=nst: v.tensor_copy(
                                     dap, slot_ap(slot, 0, nst * 128)),
                                 waits=ew)
                    a_ring[slot] = ("dve", evt)
                else:
                    evt = p.emit(eng, lambda s, dap=dap, slot=slot,
                                 nst=nst: s.copy(
                                     dap, slot_ap(slot, 0, nst * 128)),
                                 waits=ew)
                    a_ring[slot] = ("act", evt)
                t_seq[0] += 1
                a_seq[0] += 1
                return tm

            pend = [None]
            rs_tok = {0: None, 1: None}

            def flush_ptr():
                if pend[0] is None:
                    return
                qt_p, h_p, exps_p, guard_p = pend[0]
                for g4 in range((qt_p + 4) // 4):
                    emit_tr_group(qt_p, h_p, g4, exps_p, guard_p)
                pend[0] = None

            def emit_head_step(qt, h, qlw, ctx_guard):
                span_ = (qt + 1) * 128
                nch_ = (span_ + 511) // 512
                prev = pend[0]
                ngr_p = 0 if prev is None else (prev[0] + 4) // 4
                exps = []
                for j in range(max(nch_, ngr_p)):
                    mmt = mtok = None
                    if j < nch_:
                        slot = a_seq[0] % NSLOT
                        n = min(512, span_ - j * 512)
                        w0 = [a_ring[slot]]
                        if j == 0:
                            w0.append(("act", qlw))
                        mmt = p.mm(slot_ap(slot, 0, n),
                                   ql_sb[:, h * 128:(h + 1) * 128],
                                   ckvT[:, j * 512:j * 512 + n], True, True,
                                   waits=w0, inc="pe")
                        if j == nch_ - 1:
                            off = qt * 128 - j * 512
                            mtok = p.emit(
                                "vector", lambda v, slot=slot, off=off:
                                v.tensor_tensor(
                                    slot_ap(slot, off, off + 128),
                                    slot_ap(slot, off, off + 128),
                                    mask2[:], ALU.add), waits=(("pe", mmt),))
                        a_seq[0] += 1
                    trt = None
                    if prev is not None and j < ngr_p:
                        trt = emit_tr_group(prev[0], prev[1], j, prev[2],
                                            prev[3])
                    if j < nch_:
                        ew = [("pe", trt if trt is not None else mmt)]
                        if mtok is not None:
                            ew.append(("dve", mtok))
                        if bexp0[0]:
                            ew.append(("dve", bcol_tok))
                            bexp0[0] = False
                        sco = (h % 2) * 4
                        if j == 0 and rs_tok[h % 2] is not None:
                            ew.append(("dve", rs_tok[h % 2]))
                        et = p.emit(
                            "scalar",
                            lambda s, j=j, slot=slot, n=n, sco=sco:
                            s.activation(
                                ptk[:, j * 512:j * 512 + n],
                                slot_ap(slot, 0, n),
                                AF.Exp, bias=bcol[:, 0:1],
                                accum_out=sch[:, sco + j:sco + j + 1]),
                            waits=ew)
                        a_ring[slot] = ("act", et)
                        exps.append(et)
                sc = (qt % 2) * NH
                sco = (h % 2) * 4
                rs_tok[h % 2] = p.emit(
                    "vector", lambda v, h=h, nch_=nch_, sc=sc, sco=sco:
                    v.reduce_sum(
                        sumsf[:, sc + h:sc + h + 1],
                        sch[:, sco:sco + nch_], AX.X),
                    waits=(("act", exps[-1]),))
                pend[0] = (qt, h, exps, ctx_guard)

            for g in range(4):
                qlw = emit_ql_group(0, g)
                for h in range(4 * g, 4 * g + 4):
                    emit_head_step(0, h, qlw, None)

            gate_tok_prev = None
            yev_last = None
            ln2_ev = None
            for qt in range(NT):
                flush_ptr()
                chain_act = p.counts["act"]
                chain_dve = p.counts["dve"]
                ctx_ev = {}
                ctx_stop = None
                for c in range(4):
                    cc = c * 512
                    for si in range(qt + 1):
                        w0 = []
                        if si == 0:
                            w0 = [("act", chain_act), ("dve", chain_dve)]
                            if c == 0:
                                if ln2_ev is not None:
                                    w0.append(("act", ln2_ev))
                                if yev_last is not None:
                                    w0.append(("dve", yev_last))
                            else:
                                w0.append(ctx_ev[c - 1])
                        ctx_stop = p.mm(ps_c,
                                        ckvN[:, si * 128:(si + 1) * 128],
                                        p_ar[:, si * T + cc:si * T + cc + 512],
                                        si == 0, si == qt, waits=w0,
                                        inc="pe" if si == qt else None)
                    eng = "vector" if (c % 2) else "scalar"
                    if eng == "vector":
                        cev = p.emit(eng, lambda v, cc=cc: v.tensor_copy(
                            ctx_sb[:, cc:cc + 512], ps_c[:]),
                            waits=(("pe", ctx_stop),))
                        ctx_ev[c] = ("dve", cev)
                    else:
                        cev = p.emit(eng, lambda s, cc=cc: s.copy(
                            ctx_sb[:, cc:cc + 512], ps_c[:]),
                            waits=(("pe", ctx_stop),))
                        ctx_ev[c] = ("act", cev)
                    if qt + 1 < NT:
                        qlw = emit_ql_group(qt + 1, c)
                        for h in range(4 * c, 4 * c + 4):
                            emit_head_step(qt + 1, h, qlw,
                                            ("pe", ctx_stop))
                sc = (qt % 2) * NH
                rc = p.emit("vector", lambda v, sc=sc: v.reciprocal(
                    rinv16[:, 0:NH], sumsf[:, sc:sc + NH]),
                    waits=(("act", chain_act),))
                ytoks = []
                for h in range(NH):
                    psy = ps_c if h < 8 else ps_d
                    col = (h % 8) * 64
                    w0 = []
                    if h == 0:
                        w0 = [ctx_ev[0], ctx_ev[1], ctx_ev[2], ctx_ev[3]]
                    if h == 8 and gate_tok_prev is not None:
                        w0 = [("dve", gate_tok_prev)]
                    ytoks.append(p.mm(
                        psy[:, col:col + 64],
                        ctx_sb[:, h * 128:(h + 1) * 128],
                        M_sb[:, h * 64:(h + 1) * 64],
                        h == 0 or h == 8, h == 7 or h == 15,
                        waits=w0, inc="pe" if h == NH - 1 else None))
                yev = None
                for h in range(NH):
                    psy = ps_c if h < 8 else ps_d
                    col = (h % 8) * 64
                    w0 = [("pe", ytoks[-1])] if h == 0 else []
                    yev = p.emit("vector", lambda v, qt=qt, h=h, psy=psy,
                                 col=col: v.scalar_tensor_tensor(
                                     R[:, qt * C + h * 64:qt * C + (h + 1) * 64],
                                     psy[:, col:col + 64], rinv16[:, h:h + 1],
                                     R[:, qt * C + h * 64:qt * C + (h + 1) * 64],
                                     ALU.mult, ALU.add), waits=w0)
                yev_last = yev
                ln2_ev = ln_tile(qt, region="cd", trans_wait=yev)
                for ct in range(NCT):
                    w0 = []
                    if ct == 0:
                        w0 = [("act", ln2_ev), ("dve", yev)]
                    lg = p.mm(ps_d[:, 0:8], xlnT[:, ct * T + qt * 128:
                                                 ct * T + (qt + 1) * 128],
                              wr[:, ct * NE:(ct + 1) * NE], ct == 0,
                              ct == NCT - 1,
                              waits=w0, inc="pe" if ct == NCT - 1 else None)
                p.emit("vector", lambda v: v.reduce_max(gc[:, 0:1],
                                                        ps_d[:, 0:8], AX.X),
                       waits=(("pe", lg),))
                p.emit("vector", lambda v: v.tensor_scalar(
                    gmask[:], ps_d[:, 0:8], gc[:, 0:1], None, ALU.is_equal))
                p.emit("vector", lambda v: v.scalar_tensor_tensor(
                    gm[:], gmask[:], -3.0e38, ps_d[:, 0:8], ALU.mult, ALU.add))
                p.emit("vector", lambda v: v.tensor_tensor_scan(
                    gsc[:], gmask[:], gmask[:], 0.0, ALU.add, ALU.bypass))
                p.emit("vector", lambda v: v.tensor_scalar(
                    gsc[:], gsc[:], 1.0, None, ALU.is_equal))
                p.emit("vector", lambda v: v.tensor_mul(gmask[:], gmask[:],
                                                        gsc[:]))
                p.emit("vector", lambda v: v.reduce_max(gc[:, 1:2], gm[:],
                                                        AX.X))
                gd = p.emit("vector", lambda v: v.tensor_sub(gc[:, 2:3],
                                                             gc[:, 0:1],
                                                             gc[:, 1:2]))
                ga = p.emit("scalar", lambda s: s.activation(gc[:, 3:4],
                                                             gc[:, 2:3],
                                                             AF.Exp,
                                                             scale=-1.0),
                            waits=(("dve", gd),))
                p.emit("vector", lambda v: v.tensor_scalar_add(gc[:, 4:5],
                                                               gc[:, 3:4],
                                                               1.0),
                       waits=(("act", ga),))
                p.emit("vector", lambda v: v.reciprocal(gc[:, 5:6],
                                                        gc[:, 4:5]))
                gate_tok_prev = p.emit("vector", lambda v, qt=qt:
                                       v.tensor_scalar(
                                           gate[:, qt * NE:(qt + 1) * NE],
                                           gmask[:], gc[:, 5:6], None,
                                           ALU.mult))
                p.emit("vector", lambda v, qt=qt: v.reduce_sum(
                    wcol[:, qt:qt + 1], gate[:, qt * NE:(qt + 1) * NE], AX.X))
                if b == 0 and qt == 0:
                    ck(4)

            if b == 0:
                ck(5)
            att_end = dict(pe=p.counts["pe"], act=p.counts["act"],
                           dve=p.counts["dve"])
            last_add = att_end["dve"]
            p.dma("sync", rspill_d.ap(), R[:], "dsp", waits=(("dve", last_add),))
            DSP = ("dsp", p.counts["dsp"])
            if b == 0:
                ck(6)

            if debug_taps and b == 1:
                aw = (("dve", att_end["dve"]), ("act", att_end["act"]))
                p.dma("sync", taps["tap_xmid"].ap(), R[:], "dtap", waits=aw)
                p.dma("sync", taps["tap_gate"].ap(), gate[:], "dtap", waits=aw)
                p.dma("sync", taps["tap_ckvT"].ap(), ckvT[:], "dtap", waits=aw)
                p.dma("sync", taps["tap_ckvN"].ap(), ckvN[:], "dtap", waits=aw)
                p.dma("sync", taps["tap_t1T"].ap(), t1T[:], "dtap", waits=aw)
                p.dma("sync", taps["tap_xlnT"].ap(), xlnT[:], "dtap", waits=aw)
                p.dma("sync", taps["tap_p"].ap(), p_ar[:], "dtap", waits=aw)

            ATTW = [("pe", att_end["pe"]), ("dve", att_end["dve"]),
                    ("act", att_end["act"])]
            if debug_taps:
                ATTW.append(("dtap", p.counts["dtap"]))

            CAP = 512
            BIGI = 16384.0
            ATTW = [("pe", att_end["pe"]), ("dve", att_end["dve"]),
                    ("act", att_end["act"])]
            if debug_taps:
                ATTW.append(("dtap", p.counts["dtap"]))

            gtoks = []
            for tt in range(NT):
                w0 = list(ATTW) if tt == 0 else []
                gtoks.append(p.mm(
                    ps_big[0:8, tt * 128:(tt + 1) * 128],
                    gate[:, tt * NE:(tt + 1) * NE], identf[:], True, True,
                    waits=w0, inc="pe" if tt == NT - 1 else None))
            gev = None
            for chk in range(4):
                gev = p.emit("scalar", lambda s, chk=chk: s.copy(
                    gateT_sb[:, chk * 512:(chk + 1) * 512],
                    ps_big[0:8, chk * 512:(chk + 1) * 512]),
                    waits=(("pe", gtoks[-1]),) if chk == 0 else ())
            p.emit("vector", lambda v: v.tensor_scalar(
                maskr[:], gateT_sb[:], 0.0, None, ALU.is_gt),
                waits=(("act", gev),))
            p.emit("vector", lambda v: v.tensor_tensor_scan(
                scanr[:], maskr[:], maskr[:], 0.0, ALU.add, ALU.bypass))
            p.emit("vector", lambda v: v.tensor_sub(scanr[:], scanr[:],
                                                    maskr[:]))
            p.emit("vector", lambda v: v.tensor_scalar(
                rowa[:], scanr[:], 128.0, None, ALU.is_ge))
            p.emit("vector", lambda v: v.tensor_scalar(
                rowb[:], scanr[:], 256.0, None, ALU.is_ge))
            p.emit("vector", lambda v: v.tensor_add(rowa[:], rowa[:], rowb[:]))
            p.emit("vector", lambda v: v.tensor_scalar(
                rowb[:], scanr[:], 384.0, None, ALU.is_ge))
            p.emit("vector", lambda v: v.tensor_add(rowa[:], rowa[:], rowb[:]))
            p.emit("vector", lambda v: v.scalar_tensor_tensor(
                rowb[:], rowa[:], -128.0, scanr[:], ALU.mult, ALU.add))
            p.emit("vector", lambda v: v.tensor_scalar(
                rowa[:], rowa[:], ecol[0:8, 0:1], None, ALU.add))
            p.emit("vector", lambda v: v.scalar_tensor_tensor(
                rowb[:], rowb[:], 32.0, rowa[:], ALU.mult, ALU.add))
            p.emit("vector", lambda v: v.tensor_scalar(
                rowa[:], scanr[:], float(CAP), None, ALU.is_ge))
            p.emit("vector", lambda v: v.scalar_tensor_tensor(
                rowb[:], rowa[:], BIGI, rowb[:], ALU.mult, ALU.add))
            msk_tok = p.emit("vector", lambda v: v.tensor_mul(
                rowb[:], rowb[:], maskr[:]))
            cmb = []
            for chk in range(4):
                cmb.append(p.mm(ps_big[0:1, chk * 512:(chk + 1) * 512],
                                ones8[0:8, 0:1],
                                rowb[:, chk * 512:(chk + 1) * 512], True, True,
                                waits=(("dve", msk_tok),) if chk == 0 else (),
                                inc="pe" if chk == 3 else None))
            prow_ev = p.emit("scalar", lambda s: s.copy(posrow[0:1, :],
                                                        ps_big[0:1, 0:2048]),
                             waits=(("pe", cmb[-1]),))
            ptoks = []
            for tt in range(NT):
                ptoks.append(p.mm(
                    ps_big[:, 1024 + tt:1024 + tt + 1],
                    posrow[0:1, tt * 128:(tt + 1) * 128], ones1f[0:1, 0:1],
                    True, True, waits=(("act", prow_ev),) if tt == 0 else (),
                    inc="pe" if tt == NT - 1 else None))
            pf_ev = p.emit("scalar", lambda s: s.copy(
                posf[:], ps_big[:, 1024:1024 + 16]),
                waits=(("pe", ptoks[-1]),))
            pi_tok = p.emit("vector", lambda v: v.tensor_copy(posi[:], posf[:]),
                            waits=(("act", pf_ev),))
            p.emit("vector", lambda v: v.memset(idxini[:], 16384))
            ini_tok = p.counts["dve"]
            p.dma("gpsimd", idxl_d.ap(), idxini[:], "didx",
                  waits=(("dve", ini_tok),) + tuple(ATTW))
            ini_done = p.counts["didx"]
            for tt in range(NT):
                def fn(g, tt=tt):
                    return g.indirect_dma_start(
                        bass.AP(idxl_d, 0, [[1, 4096], [1, 1]]),
                        bass.IndirectOffsetOnAxis(posi[:, tt:tt + 1], 0),
                        iota_sb[:, tt:tt + 1], None,
                        bounds_check=REGS["b4095"], oob_is_err=False)
                p.emit("gpsimd", fn,
                       waits=(("dve", pi_tok), ("didx", ini_done))
                       if tt == 0 else (),
                       inc="didx", n=16)
            scat_done = p.counts["didx"]
            p.dma("gpsimd", idx_sb[:], idxl_d.ap(), "didx",
                  waits=(("didx", scat_done),))
            idx_done = p.counts["didx"]
            z_tok = p.emit("vector", lambda v: v.memset(yld[:, 0:1024], 0))
            for tt in range(NT):
                p.dma("sync", bass.AP(ytok_d, tt * 128 * C, [[C, 128], [1, C]]),
                      yld[:, 0:1024], "dz0",
                      waits=(("dve", z_tok),) if tt == 0 else ())
            dz_done = p.counts["dz0"]
            if b == 0:
                ck(6.2)

            w1_free = [None, None]
            w2_free = [None, None]
            gx_free = [None, None]
            gxT_free = [None, None]
            gxtr_ring = [None, None, None, None]
            gxtr_seq = [0]
            ytr_free = None
            ytr_free2 = [(), ()]
            ybf_free = None
            yevh = {}
            gelu_hist = {}
            h_seq = 0
            y_seq = 0
            ytt_use = [0, 0]
            sc_count = 0
            for e in range(NE):
                DZW = ("dz0", dz_done) if e == 0 else None
                gxw = gxT_free[e % 2]
                gx_ev = None
                for g4 in range(4):
                    slot = g4 % 2
                    mz = p.emit("vector", lambda v, slot=slot: v.memset(
                        gx_tok[:, slot * 1024:(slot + 1) * 1024], 0),
                        waits=(("pe", gx_free[slot]),)
                        if gx_free[slot] is not None else (DSP,))
                    dw = [("didx", idx_done), ("dxt", p.counts["dxt"]),
                          ("dve", mz)]
                    def gfn(gp, e=e, g4=g4, slot=slot):
                        return gp.indirect_dma_start(
                            gx_tok[:, slot * 1024:(slot + 1) * 1024], None,
                            bass.AP(xtok_d, 0, [[C, T], [1, C]]),
                            bass.IndirectOffsetOnAxis(
                                idx_sb[:, e * 4 + g4:e * 4 + g4 + 1], 0),
                            bounds_check=REGS["b2047"], oob_is_err=False)
                    gx_uses[slot] += 1
                    p.emit("gpsimd", gfn, waits=dw, inc="dgx%d" % slot, n=16)
                    guse = ("dgx%d" % slot, 16 * gx_uses[slot])
                    tok = None
                    for half in range(2):
                        gb = gxtr_seq[0] % 4
                        for k in range(4):
                            ct = half * 4 + k
                            w0 = []
                            if k == 0:
                                if gxtr_ring[gb] is not None:
                                    w0.append(gxtr_ring[gb])
                                if half == 0:
                                    w0.append(guse)
                                    if gxw is not None:
                                        w0.append(gxw)
                                        gxw = None
                                    if g4 == 0 and ytr_free is not None:
                                        w0.append(ytr_free)
                            tok = p.mm(
                                ps_big[:, gb * 512 + k * 128:
                                       gb * 512 + (k + 1) * 128],
                                gx_tok[:, slot * 1024 + ct * 128:
                                       slot * 1024 + (ct + 1) * 128],
                                identb[:], True, True, waits=w0,
                                inc="pe" if k == 3 else None)
                        dap = bass.AP(gxT, (e % 2) * 4096 +
                                      half * 4 * 512 + g4 * 128,
                                      [[2 * NCT * 512, 128], [512, 4],
                                       [1, 128]])
                        eng = "vector" if gxtr_seq[0] % 2 else "scalar"
                        if eng == "vector":
                            ev = p.emit(eng, lambda v, dap=dap, gb=gb:
                                        v.tensor_copy(
                                            dap,
                                            ps_big[:, gb * 512:
                                                   gb * 512 + 512]),
                                        waits=(("pe", tok),))
                            gxtr_ring[gb] = ("dve", ev)
                        else:
                            ev = p.emit(eng, lambda s, dap=dap, gb=gb:
                                        s.copy(
                                            dap,
                                            ps_big[:, gb * 512:
                                                   gb * 512 + 512]),
                                        waits=(("pe", tok),))
                            gxtr_ring[gb] = ("act", ev)
                        gxtr_seq[0] += 1
                    gx_free[slot] = tok
                gx_ev = (p.counts["act"], p.counts["dve"])
                if b == 0 and e == 0:
                    ck(6.5)

                for htg in range(8):
                    slot = htg % 2
                    w1_use[slot] += 1
                    dw = []
                    if w1_free[slot] is not None:
                        dw.append(("pe", w1_free[slot]))
                    if b == 0 and e == 0 and htg < 2:
                        dw += ATTW
                    if b == 0 and e == 0:
                        dw.append(DCW)
                    p.dma("sync", w1ring[:, slot * 4096:(slot + 1) * 4096],
                          bass.AP(w1bf_d, (e * 8 + htg) * 128 * 4096,
                                  [[4096, 128], [1, 4096]]),
                          "w1%d" % slot, waits=dw)
                    for htl in range(4):
                        ht = htg * 4 + htl
                        reg = h_seq % 2
                        psh = ps_a if reg == 0 else ps_b
                        toks = []
                        for ct in range(NCT):
                            w0 = []
                            if ct == 0:
                                if htl == 0:
                                    w0.append(("w1%d" % slot,
                                               16 * w1_use[slot]))
                                if ht == 0:
                                    w0 += [("act", gx_ev[0]),
                                           ("dve", gx_ev[1])]
                                prev = gelu_hist.get(h_seq - 2)
                                if prev is not None:
                                    w0.append(("act", prev))
                            toks.append(p.mm(
                                psh,
                                w1ring[:, slot * 4096 + ct * 512 +
                                       htl * 128:slot * 4096 + ct * 512 +
                                       (htl + 1) * 128],
                                gxT[:, (e % 2) * 4096 + ct * 512:
                                    (e % 2) * 4096 + (ct + 1) * 512],
                                ct == 0, ct == NCT - 1,
                                waits=w0, inc="pe" if ct == NCT - 1 else None))
                        gw2 = [("pe", toks[-1])]
                        if b == 0 and e == 0 and ht == 0:
                            gw2.append(DSP)
                        gelu_hist[h_seq] = p.emit(
                            "scalar", lambda s, ht=ht, psh=psh: s.activation(
                                h_g[:, ht * 512:(ht + 1) * 512], psh[:],
                                AF.Gelu), waits=gw2)
                        h_seq += 1
                        if htl == 3:
                            w1_free[slot] = toks[-1]

                ybf_ev = None
                for ct in range(NCT):
                    slot = ct % 2
                    w2_use[slot] += 1
                    dw = []
                    if w2_free[slot] is not None:
                        dw.append(("pe", w2_free[slot]))
                    if b == 0 and e == 0 and ct < 2:
                        dw += ATTW
                    if b == 0 and e == 0:
                        dw.append(DCW)
                    p.dma("sync", w2ring[:, slot * 4096:(slot + 1) * 4096],
                          bass.AP(w2bf_d, (e * 8 + ct) * 128 * 4096,
                                  [[4096, 128], [1, 4096]]),
                          "w2%d" % slot, waits=dw)
                    reg = y_seq % 2
                    psy = ps_c if reg == 0 else ps_d
                    toks = []
                    for ht in range(NHT):
                        w0 = []
                        if ht == 0:
                            w0.append(("w2%d" % slot, 16 * w2_use[slot]))
                            w0.append(("act", gelu_hist[h_seq - 1]))
                            prev = yevh.get(y_seq - 2)
                            if prev is not None:
                                w0.append(prev)
                            if ct == 0 and ybf_free is not None:
                                w0.append(("pe", ybf_free))
                        toks.append(p.mm(
                            psy,
                            w2ring[:, slot * 4096 + ht * 128:
                                   slot * 4096 + (ht + 1) * 128],
                            h_g[:, ht * 512:(ht + 1) * 512],
                            ht == 0, ht == NHT - 1,
                            waits=w0, inc="pe" if ht == NHT - 1 else None))
                    eng = "vector" if ct % 2 else "scalar"
                    if eng == "vector":
                        ev = p.emit(eng, lambda v, ct=ct, psy=psy:
                                    v.tensor_copy(
                                        ybf[:, ct * 512:(ct + 1) * 512],
                                        psy[:]), waits=(("pe", toks[-1]),))
                        yevh[y_seq] = ("dve", ev)
                    else:
                        ev = p.emit(eng, lambda s, ct=ct, psy=psy: s.copy(
                            ybf[:, ct * 512:(ct + 1) * 512], psy[:]),
                            waits=(("pe", toks[-1]),))
                        yevh[y_seq] = ("act", ev)
                    ybf_ev = yevh[y_seq]
                    y_seq += 1
                    w2_free[slot] = toks[-1]

                ybf_ev2 = yevh.get(y_seq - 2)
                ytr_last = None
                for g4 in range(4):
                    odd = g4 % 2
                    toks = []
                    for ct in range(NCT):
                        w0 = []
                        if ct == 0:
                            w0.append(ybf_ev)
                            if g4 == 0 and ybf_ev2 is not None:
                                w0.append(ybf_ev2)
                            for t_ in ytr_free2[odd]:
                                w0.append(t_)
                        if odd:
                            dst = (ps_a if ct < 4 else ps_b)[
                                :, (ct % 4) * 128:(ct % 4 + 1) * 128]
                        else:
                            dst = ps_big[:, 1024 + ct * 128:
                                         1024 + (ct + 1) * 128]
                        toks.append(p.mm(
                            dst,
                            ybf[:, ct * 512 + g4 * 128:ct * 512 +
                                g4 * 128 + 128],
                            identb[:], True, True,
                            waits=w0, inc="pe" if ct == NCT - 1 else None))
                    slot = (e * 4 + g4) % 2
                    ytt_use[slot] += 1
                    ew = [("pe", toks[-1])]
                    if sc_uses[slot] > 0:
                        ew.append(("dsc%d" % slot, 16 * sc_uses[slot]))
                    if odd:
                        yt_a = p.emit("scalar", lambda s, slot=slot: s.copy(
                            ytt[:, slot * 1024:slot * 1024 + 512],
                            ps_a[:]), waits=ew)
                        yt_d = p.emit("vector", lambda v, slot=slot:
                                      v.tensor_copy(
                                          ytt[:, slot * 1024 + 512:
                                              (slot + 1) * 1024],
                                          ps_b[:]), waits=ew)
                        ytr_free2[1] = (("act", yt_a), ("dve", yt_d))
                        scw = (("act", yt_a), ("dve", yt_d), DZW)
                    else:
                        yt_ev = p.emit("scalar", lambda s, slot=slot: s.copy(
                            ytt[:, slot * 1024:(slot + 1) * 1024],
                            ps_big[:, 1024:2048]), waits=ew)
                        ytr_free2[0] = (("act", yt_ev),)
                        ytr_free = ("act", yt_ev)
                        scw = (("act", yt_ev), DZW)
                    ytr_last = toks[-1]
                    def sfn(gp, e=e, g4=g4, slot=slot):
                        return gp.indirect_dma_start(
                            bass.AP(ytok_d, 0, [[C, T], [1, C]]),
                            bass.IndirectOffsetOnAxis(
                                idx_sb[:, e * 4 + g4:e * 4 + g4 + 1], 0),
                            ytt[:, slot * 1024:(slot + 1) * 1024], None,
                            bounds_check=REGS["b2047"], oob_is_err=False)
                    sc_uses[slot] += 1
                    p.emit("gpsimd", sfn, waits=scw,
                           inc="dsc%d" % slot, n=16)
                    sc_count += 1
                ybf_free = ytr_last
                gxT_free[e % 2] = ("pe", toks[-1])
                if debug_taps and b == 1 and e == 0:
                    tw = (("act", p.counts["act"]), ("dve", p.counts["dve"]),
                          ("didx", p.counts["didx"]))
                    p.dma("sync", taps["tap_idx"].ap(), idx_sb[:], "dtap",
                          waits=tw)
                    p.dma("sync", taps["tap_gxT"].ap(),
                          gxT[:, 0:NCT * 512], "dtap", waits=tw)
                    p.dma("sync", taps["tap_ybf"].ap(), ybf[:], "dtap",
                          waits=tw)
                    p.dma("sync", taps["tap_hg"].ap(), h_g[:], "dtap",
                          waits=tw)
                if b == 0 and e == 0:
                    ck(7)

            if b == 0:
                ck(7.5)
            rl_free = [None, None]
            yld_free = [None, None]
            out_free = [None, None]
            rl_use2 = [0, 0]
            yld_use = [0, 0]
            sc_all0 = p.counts["dsc0"]
            sc_all1 = p.counts["dsc1"]
            for tt in range(NT):
                slot = tt % 2
                rl_use[slot] += 1
                rl_use2[slot] += 1
                dw = [DSP]
                if rl_free[slot] is not None:
                    dw.append(("dve", rl_free[slot]))
                if tt < 2:
                    dw += ATTW
                p.dma("sync", rl_buf[:, slot * 1024:(slot + 1) * 1024],
                      bass.AP(rspill_d, tt * 1024, [[NT * C, 128], [1, 1024]]),
                      "drl%d" % slot, waits=dw)
                yld_use[slot] += 1
                dw2 = [("dsc0", sc_all0), ("dsc1", sc_all1)]
                if yld_free[slot] is not None:
                    dw2.append(("dve", yld_free[slot]))
                if fvar == 2:
                    dw2 = [w for w in dw2 if w[0] != "dsc"]
                if fvar != 3:
                    p.dma("sync", yld[:, slot * 1024:(slot + 1) * 1024],
                          bass.AP(ytok_d, tt * 128 * C, [[C, 128], [1, C]]),
                          "dyl", waits=dw2)
                yl_tok = p.counts["dyl"]
                ow = [("drl%d" % slot, 16 * rl_use[slot])]
                if fvar != 3:
                    ow.append(("dyl", yl_tok))
                if out_free[slot] is not None:
                    ow.append(("dout%d" % slot, out_free[slot]))
                if fvar in (1, 3):
                    at_ = p.emit("vector", lambda v, slot=slot:
                                 v.tensor_copy(
                                     out_st[:, slot * 1024:(slot + 1) * 1024],
                                     rl_buf[:, slot * 1024:(slot + 1) * 1024]),
                                 waits=ow)
                else:
                    at_ = p.emit("vector", lambda v, slot=slot, tt=tt:
                                 v.scalar_tensor_tensor(
                                     out_st[:, slot * 1024:(slot + 1) * 1024],
                                     yld[:, slot * 1024:(slot + 1) * 1024],
                                     wcol[:, tt:tt + 1],
                                     rl_buf[:, slot * 1024:(slot + 1) * 1024],
                                     ALU.mult, ALU.add), waits=ow)
                rl_free[slot] = at_
                yld_free[slot] = at_
                p.dma("sync", bass.AP(out_d, b * T * C + tt * 128 * C,
                                      [[C, 128], [1, C]]),
                      out_st[:, slot * 1024:(slot + 1) * 1024],
                      "dout%d" % slot, waits=(("dve", at_),))
                out_free[slot] = p.counts["dout%d" % slot]

            prev_end = dict(pe=p.counts["pe"], act=p.counts["act"],
                            dve=p.counts["dve"])
            prev_dout = (p.counts["dout0"], p.counts["dout1"])
            if b == 0:
                ck(8)

    except _Stop:
        pass

    fw = [(sname, p.counts[sname]) for sname in p.semnames]
    p.emit("sync", lambda e: None, waits=fw)

    semh = {}
    for name in p.semnames:
        semh[name] = es.enter_context(nc.semaphore(name))

    with nc.Block() as block:
        def runner(engname):
            def run_ops(eng):
                for fn, waits, inc, n in p.ops[engname]:
                    for s, v in waits:
                        eng.wait_ge(semh[s], v)
                    inst = fn(eng)
                    if inc is not None:
                        assert inst is not None, engname
                        inst.then_inc(semh[inc], n)

            def body(eng):
                if engname == "gpsimd":
                    with eng.register("rb2047") as r1, \
                            eng.register("rb4095") as r2:
                        eng.reg_mov(r1, 2047)
                        eng.reg_mov(r2, 4095)
                        REGS["b2047"] = r1
                        REGS["b4095"] = r2
                        run_ops(eng)
                else:
                    run_ops(eng)
            return body
        block.tensor(runner("tensor"))
        block.vector(runner("vector"))
        block.scalar(runner("scalar"))
        block.sync(runner("sync"))
        block.gpsimd(runner("gpsimd"))

    es.close()
    return nc, taps


_BF = ml_dtypes.bfloat16


def _consts():
    ii = np.arange(128)
    return dict(
        identf=np.eye(128, dtype=np.float32),
        identb=np.eye(128, dtype=_BF),
        ones1f=np.ones((1, 128), dtype=np.float32),
        ones1b=np.ones((1, 128), dtype=_BF),
        onescol=np.ones((128, 1), dtype=_BF),
        mask2=np.where(ii[None, :] <= ii[:, None], 0.0, NEGC).astype(np.float32),
        ones8=np.ones((8, 1), dtype=np.float32),
        ecol=(4.0 * np.arange(8, dtype=np.float32))[:, None].copy(),
        iota_tok=(np.arange(16, dtype=np.int32)[None, :] * 128
                  + np.arange(128, dtype=np.int32)[:, None]).copy(),
    )


def kernel(L0_attn_x, L0_attn_W_dq, L0_attn_W_uq, L0_attn_W_dkv, L0_attn_W_uk,
           L0_attn_W_uv, L0_attn_W_o, L0_ffn_Wr, L0_ffn_W1, L0_ffn_W2):
    x = np.ascontiguousarray(np.asarray(L0_attn_x, dtype=np.float32))
    weights = dict(
        wdq=np.ascontiguousarray(L0_attn_W_dq, dtype=np.float32),
        wuq=np.ascontiguousarray(L0_attn_W_uq, dtype=np.float32),
        wdkv=np.ascontiguousarray(L0_attn_W_dkv, dtype=np.float32),
        wuk=np.ascontiguousarray(L0_attn_W_uk, dtype=np.float32),
        wuv=np.ascontiguousarray(L0_attn_W_uv, dtype=np.float32),
        wo=np.ascontiguousarray(L0_attn_W_o, dtype=np.float32),
        wr=np.ascontiguousarray(L0_ffn_Wr, dtype=np.float32),
        w1=np.ascontiguousarray(
            np.asarray(L0_ffn_W1, dtype=np.float32)
            .reshape(NE, 8, 128, 8, 512).transpose(0, 3, 2, 1, 4)
            .reshape(NE * 8, 128, 4096)),
        w2=np.ascontiguousarray(
            np.asarray(L0_ffn_W2, dtype=np.float32)
            .reshape(NE, 32, 128, 8, 128).transpose(0, 3, 2, 1, 4)
            .reshape(NE * 8, 128, 4096)),
    )
    weights.update(_consts())
    nc, _ = build(debug_taps=False)
    in_maps = []
    for i in range(NCORES):
        m = dict(weights)
        m["x"] = x[i * NB:(i + 1) * NB]
        in_maps.append(m)
    res = run_bass_kernel_spmd(nc, in_maps, core_ids=list(range(NCORES)))
    out = np.concatenate([r["out"] for r in res.results], axis=0)
    return out


# version 31
# speedup vs baseline: 1.1859x; 1.1859x over previous
import numpy as np
import ml_dtypes
import concourse.bass as bass
import concourse.mybir as mybir
from concourse.bass_utils import run_bass_kernel_spmd
from contextlib import ExitStack

F32 = mybir.dt.float32
BF16 = mybir.dt.bfloat16
AF = mybir.ActivationFunctionType
ALU = mybir.AluOpType
AX = mybir.AxisListType

NCORES = 8
NB = 2
T = 2048
C = 1024
NT = 16
NCT = 8
NH = 16
NE = 8
HID = 4096
NHT = 32
EPS = 1e-5
NEGC = -30000.0

ENGINES = ("tensor", "vector", "scalar", "sync", "gpsimd")


class Prog:
    def __init__(self):
        self.ops = {e: [] for e in ENGINES}
        self.counts = {}
        self.semnames = []

    def sem(self, name):
        self.semnames.append(name)
        self.counts[name] = 0

    SELF_WAIT = True

    def emit(self, eng, fn, waits=(), inc=None, n=1):
        waits = [w for w in waits
                 if w is not None and w[1] is not None and w[1] > 0]
        if eng == "vector":
            if self.SELF_WAIT and self.counts["dve"] > 0:
                waits.append(("dve", self.counts["dve"]))
            inc, n = "dve", 1
        elif eng == "scalar":
            if self.SELF_WAIT and self.counts["act"] > 0:
                waits.append(("act", self.counts["act"]))
            inc, n = "act", 1
        tok = None
        if inc is not None:
            self.counts[inc] += n
            tok = self.counts[inc]
        self.ops[eng].append((fn, tuple(waits), inc, n))
        return tok

    def mm(self, out, lhsT, rhs, start, stop, waits=(), inc=None):
        out = out if isinstance(out, bass.AP) else out[:]
        lhsT = lhsT if isinstance(lhsT, bass.AP) else lhsT[:]
        rhs = rhs if isinstance(rhs, bass.AP) else rhs[:]

        def fn(t, out=out, lhsT=lhsT, rhs=rhs, start=start, stop=stop):
            return t.matmul(out, lhsT, rhs, start=start, stop=stop,
                            skip_group_check=True)
        return self.emit("tensor", fn, waits, inc=inc)

    def dma(self, queue, dst, src, sem, waits=()):
        def fn(e, dst=dst, src=src):
            return e.dma_start(dst, src)
        return self.emit(queue, fn, waits, inc=sem, n=16)


class _Stop(Exception):
    pass


REGS = {}


def build(debug_taps=False, stage=99, fvar=0):
    nc = bass.Bass("TRN2", target_bir_lowering=False)
    p = Prog()

    def ck(k):
        if stage <= k:
            raise _Stop()

    x_d = nc.dram_tensor("x", [NB, T, C], F32, kind="ExternalInput")
    wdq_d = nc.dram_tensor("wdq", [128, C], F32, kind="ExternalInput")
    wuq_d = nc.dram_tensor("wuq", [C, 128], F32, kind="ExternalInput")
    wdkv_d = nc.dram_tensor("wdkv", [128, C], F32, kind="ExternalInput")
    wuk_d = nc.dram_tensor("wuk", [C, 128], F32, kind="ExternalInput")
    wuv_d = nc.dram_tensor("wuv", [C, 128], F32, kind="ExternalInput")
    wo_d = nc.dram_tensor("wo", [C, C], F32, kind="ExternalInput")
    wr_d = nc.dram_tensor("wr", [C, NE], F32, kind="ExternalInput")
    w1_d = nc.dram_tensor("w1", [NE * 8, 128, 4096], F32, kind="ExternalInput")
    w2_d = nc.dram_tensor("w2", [NE * 8, 128, 4096], F32, kind="ExternalInput")
    identf_d = nc.dram_tensor("identf", [128, 128], F32, kind="ExternalInput")
    identb_d = nc.dram_tensor("identb", [128, 128], BF16, kind="ExternalInput")
    ones1f_d = nc.dram_tensor("ones1f", [1, 128], F32, kind="ExternalInput")
    ones1b_d = nc.dram_tensor("ones1b", [1, 128], BF16, kind="ExternalInput")
    onescol_d = nc.dram_tensor("onescol", [128, 1], BF16, kind="ExternalInput")
    mask2_d = nc.dram_tensor("mask2", [128, 128], F32, kind="ExternalInput")
    ones8_d = nc.dram_tensor("ones8", [8, 1], F32, kind="ExternalInput")
    ecol_d = nc.dram_tensor("ecol", [8, 1], F32, kind="ExternalInput")
    iota_d = nc.dram_tensor("iota_tok", [128, NT], mybir.dt.int32,
                            kind="ExternalInput")
    out_d = nc.dram_tensor("out", [NB, T, C], F32, kind="ExternalOutput")
    rspill_d = nc.dram_tensor("rspill", [128, NT * C], F32)
    w1bf_d = nc.dram_tensor("w1bf", [NE * 8, 128, 4096], BF16)
    w2bf_d = nc.dram_tensor("w2bf", [NE * 8, 128, 4096], BF16)
    xtok_d = nc.dram_tensor("xtok", [T, C], BF16)
    ytok_d = nc.dram_tensor("ytok", [T, C], BF16)
    idxl_d = nc.dram_tensor("idxl", [128, 32], mybir.dt.int32)

    taps = {}
    if debug_taps:
        for name, shape, dt in (
            ("tap_xmid", [128, NT * C], F32),
            ("tap_gate", [128, NT * NE], F32),
            ("tap_ckvT", [128, T], BF16),
            ("tap_ckvN", [128, T], BF16),
            ("tap_t1T", [128, T], BF16),
            ("tap_xlnT", [128, NCT * T], BF16),
            ("tap_p", [128, NT * T], BF16),
            ("tap_Psb", [128, NH * 128], BF16),
            ("tap_idx", [128, 32], mybir.dt.int32),
            ("tap_gxT", [128, NCT * 512], BF16),
            ("tap_ybf", [128, NCT * 512], BF16),
            ("tap_hg", [128, NHT * 512], BF16),
            ("tap_Msb", [128, C], BF16),
        ):
            taps[name] = nc.dram_tensor(name, shape, dt, kind="ExternalOutput")

    for s in ("dc", "dcg", "dcw", "dx", "dsp", "drl0", "drl1", "dout0", "dout1", "dtap",
              "w10", "w11", "w20", "w21", "pe", "act", "dve",
              "dxt", "didx", "dgx0", "dgx1", "dsc0", "dsc1", "dyl", "dz0"):
        p.sem(s)

    es = ExitStack()
    sbuf = lambda name, shape, dt: es.enter_context(
        nc.sbuf_tensor(name + "_s", shape, dt))

    identf = sbuf("identf", [128, 128], F32)
    identb = sbuf("identb", [128, 128], BF16)
    ones1f = sbuf("ones1f", [1, 128], F32)
    ones1b = sbuf("ones1b", [1, 128], BF16)
    onescol = sbuf("onescol", [128, 1], BF16)
    mask2 = sbuf("mask2", [128, 128], F32)
    wdq = sbuf("wdq", [128, C], BF16)
    wdkv = sbuf("wdkv", [128, C], BF16)
    wr = sbuf("wr", [128, NCT * NE], BF16)
    wdqT = sbuf("wdqT", [128, NCT * 128], BF16)
    wdkvT = sbuf("wdkvT", [128, NCT * 128], BF16)
    P_sb = sbuf("P_sb", [128, NH * 128], BF16)
    M_sb = sbuf("M_sb", [128, C], BF16)
    xlnT = sbuf("xlnT", [128, NCT * T], BF16)
    gate = sbuf("gate", [128, NT * NE], F32)
    lnc = sbuf("lnc", [128, 8], F32)
    gmask = sbuf("gmask", [128, NE], F32)
    gm = sbuf("gm", [128, NE], F32)
    gc = sbuf("gc", [128, 8], F32)
    gsc = sbuf("gsc", [128, NE], F32)
    mch = sbuf("mch", [128, 8], F32)
    sch = sbuf("sch", [128, 8], F32)
    ones8 = sbuf("ones8", [8, 1], F32)
    ecol = sbuf("ecol", [8, 1], F32)
    iota_sb = sbuf("iota_sb", [128, NT], mybir.dt.int32)
    posf = sbuf("posf", [128, 16], F32)
    posi = sbuf("posi", [128, 16], mybir.dt.int32)
    idxini = sbuf("idxini", [128, 32], mybir.dt.int32)
    idx_sb = sbuf("idx_sb", [128, 32], mybir.dt.int32)
    wcol = sbuf("wcol", [128, NT], F32)
    mnegf = sbuf("mnegf", [128, NH], F32)
    bcol = sbuf("bcol", [128, 1], F32)
    sumsf = sbuf("sumsf", [128, 2 * NH], F32)
    rinv16 = sbuf("rinv16", [128, NH], F32)

    AR = 158 * 1024
    arena = sbuf("arena", [128, AR], mybir.dt.uint8)
    base = nc.lookup_mloc(arena).addr
    _cur = [0]

    def at(name, shape, dt, off=None):
        nbytes = int(np.prod(shape[1:])) * mybir.dt.size(dt)
        if off is None:
            off = _cur[0]
        off = (off + 31) // 32 * 32
        assert off + nbytes <= AR, (name, off, nbytes)
        h = nc.alloc_sbuf_tensor_at(name, shape, dt, offset=base + off)
        _cur[0] = off + nbytes
        return h

    h_g = at("h_g", [128, NHT * 512], BF16, 0)
    gxT = at("gxT", [128, 2 * NCT * 512], BF16)
    gx_tok = at("gx_tok", [128, 2 * 1024], BF16)
    ybf = at("ybf", [128, NCT * 512], BF16)
    ytt = at("ytt", [128, 2 * 1024], BF16)
    w1ring = at("w1ring", [128, 2 * 4096], BF16)
    w2ring = at("w2ring", [128, 2 * 4096], BF16)
    gateT_sb = at("gateT_sb", [8, T], F32)
    maskr = at("maskr", [8, T], F32)
    scanr = at("scanr", [8, T], F32)
    rowa = at("rowa", [8, T], F32)
    rowb = at("rowb", [8, T], F32)
    yld = at("yld", [128, 2 * 1024], BF16)
    posrow = gateT_sb
    out_st = at("out_st", [128, 2 * 1024], F32)
    rl_buf = at("rl_buf", [128, 2 * 1024], F32)

    _cur[0] = 0
    R = at("R", [128, NT * C], F32, 0)
    p_ar = at("p_ar", [128, NT * T], BF16)
    ql_sb = at("ql_sb", [128, T], BF16)
    t1T = at("t1T", [128, T], BF16)
    ckvT = at("ckvT", [128, T], BF16)
    ckvN = at("ckvN", [128, T], BF16)
    ctx_sb = at("ctx_sb", [128, T], BF16)
    sq_sb = at("sq_sb", [128, C], F32)
    xscr = at("xscr", [128, C], BF16)
    ptk = at("ptk", [128, T], BF16)

    _cur[0] = 0
    wo_sb = at("wo_sb", [128, NCT * C], BF16, 0)
    woT_sb = at("woT_sb", [128, NCT * C], BF16)
    a1_sb = at("a1_sb", [128, 128], BF16)
    k_sb = at("k_sb", [128, NCT * 128], BF16)
    wuq = at("wuq", [128, NCT * 128], BF16)
    wuk = at("wuk", [128, NCT * 128], BF16)
    wuv = at("wuv", [128, NCT * 128], BF16)


    ps_big = es.enter_context(nc.psum_tensor("ps_big", [128, 2048], F32))
    ps_a = es.enter_context(nc.psum_tensor("ps_a", [128, 512], F32))
    ps_b = es.enter_context(nc.psum_tensor("ps_b", [128, 512], F32))
    ps_c = es.enter_context(nc.psum_tensor("ps_c", [128, 512], F32))
    ps_d = es.enter_context(nc.psum_tensor("ps_d", [128, 512], F32))

    try:
        p.dma("sync", identf[:], identf_d.ap(), "dc")
        p.dma("sync", identb[:], identb_d.ap(), "dc")
        p.dma("sync", ones1f[:], ones1f_d.ap(), "dc")
        p.dma("sync", ones1b[:], ones1b_d.ap(), "dc")
        p.dma("sync", onescol[:], onescol_d.ap(), "dc")
        p.dma("sync", mask2[:], mask2_d.ap(), "dc")
        p.dma("gpsimd", wdq[:], wdq_d.ap(), "dcg")
        p.dma("gpsimd", wdkv[:], wdkv_d.ap(), "dcg")
        for d, dst in ((wuq_d, wuq), (wuk_d, wuk), (wuv_d, wuv)):
            p.dma("gpsimd", dst[:],
                  bass.AP(d, 0, [[128, 128], [128 * 128, NCT], [1, 128]]),
                  "dcg")
        p.dma("gpsimd", wr[:],
              bass.AP(wr_d, 0, [[NE, 128], [128 * NE, NCT], [1, NE]]), "dcg")
        p.dma("sync", ones8[:], ones8_d.ap(), "dc")
        p.dma("sync", ecol[:], ecol_d.ap(), "dc")
        p.dma("sync", iota_sb[:], iota_d.ap(), "dc")
        p.dma("gpsimd", wo_sb[:],
              bass.AP(wo_d, 0, [[C, 128], [128 * C, NCT], [1, C]]), "dcg")
        DC = ("dc", p.counts["dc"])
        DCG = ("dcg", p.counts["dcg"])
        bcol_tok = p.emit("vector", lambda v: v.memset(bcol[:], -12.0))
        bexp0 = [True]
        ck(0)

        tok = None
        for jt in range(NCT):
            tok = p.mm(ps_big[:, 0:128], wuq[:, jt * 128:(jt + 1) * 128],
                       wuk[:, jt * 128:(jt + 1) * 128], jt == 0, jt == NCT - 1,
                       waits=(DC, DCG) if jt == 0 else (),
                       inc="pe" if jt == NCT - 1 else None)
        a1_tok = p.emit("scalar", lambda s: s.copy(a1_sb[:], ps_big[:, 0:128]),
                        waits=(("pe", tok),))
        ck(0.2)
        for ct in range(NCT):
            tok = p.mm(ps_big[:, 512 + ct * 128:512 + (ct + 1) * 128],
                       wdq[:, ct * 128:(ct + 1) * 128], a1_sb[:], True, True,
                       waits=(("act", a1_tok),) if ct == 0 else (), inc="pe")
        k_tok = p.emit("scalar", lambda s: s.copy(k_sb[:], ps_big[:, 512:1536]),
                       waits=(("pe", tok),))
        ck(0.3)
        last_evac = k_tok
        ph_order = [h for h in range(NH) if h % 2 == 0] + [None] + \
                   [h for h in range(NH) if h % 2 == 1]
        k = 0
        for h in ph_order:
            if h is None:
                p.mm(ps_big[:, 0:128], wuq[:, 0:128], wuk[:, 0:128],
                     True, True)
                continue
            jt, off = h // 2, (h % 2) * 64
            tok = p.mm(ps_big[:, 1536 + (k % 4) * 128:1536 + (k % 4 + 1) * 128],
                       wuq[off:off + 64, jt * 128:(jt + 1) * 128],
                       k_sb[off:off + 64, jt * 128:(jt + 1) * 128], True, True,
                       waits=(("act", last_evac),) if k % 4 == 0 else (),
                       inc="pe")
            if k % 4 == 3:
                grp = ([h2 for h2 in ph_order if h2 is not None]
                       [k - 3:k + 1])
                for j, hh in enumerate(grp):
                    last_evac = p.emit("scalar", lambda s, hh=hh, j=j:
                                       s.activation(
                                           P_sb[:, hh * 128:(hh + 1) * 128],
                                           ps_big[:, 1536 + j * 128:
                                                  1536 + (j + 1) * 128],
                                           AF.Copy, scale=0.125),
                                       waits=(("pe", tok),) if j == 0 else ())
            k += 1
        ck(0.4)
        blocks = [(ct, jt) for ct in range(NCT) for jt in range(NCT)]
        evac_hist = [last_evac, last_evac]
        for gi in range(0, 64, 4):
            reg = (gi // 4) % 2
            c0 = reg * 512
            toks = []
            for k, (ct, jt) in enumerate(blocks[gi:gi + 4]):
                toks.append(p.mm(
                    ps_big[:, c0 + k * 128:c0 + (k + 1) * 128],
                    wo_sb[:, ct * C + jt * 128:ct * C + (jt + 1) * 128],
                    identb[:], True, True,
                    waits=(("act", evac_hist[reg]),) if k == 0 else (), inc="pe"))
            for k, (ct, jt) in enumerate(blocks[gi:gi + 4]):
                e = p.emit("scalar", lambda s, ct=ct, jt=jt, c0=c0, k=k: s.copy(
                    woT_sb[:, jt * C + ct * 128:jt * C + (ct + 1) * 128],
                    ps_big[:, c0 + k * 128:c0 + (k + 1) * 128]),
                    waits=(("pe", toks[k]),))
            evac_hist[reg] = e
        for jt in range(NCT):
            w0 = (("act", evac_hist[0]), ("act", evac_hist[1])) if jt == 0 else ()
            p.mm(ps_big[:, 0:512], wuv[:, jt * 128:(jt + 1) * 128],
                 woT_sb[:, jt * C:jt * C + 512], jt == 0, jt == NCT - 1, waits=w0)
            tok = p.mm(ps_big[:, 512:1024], wuv[:, jt * 128:(jt + 1) * 128],
                       woT_sb[:, jt * C + 512:jt * C + 1024], jt == 0, jt == NCT - 1,
                       inc="pe" if jt == NCT - 1 else None)
        ck(0.5)
        m_tok = p.emit("scalar", lambda s: s.copy(M_sb[:], ps_big[:, 0:1024]),
                       waits=(("pe", tok),))
        ck(0.6)
        evac_hist = [m_tok, m_tok]
        for i, (src, dst) in enumerate(
                [(wdq, wdqT)] * 0 + [(wdq, wdqT), (wdkv, wdkvT)]):
            for ct in range(NCT):
                reg = ct % 2
                c0 = 1024 + reg * 512
                tok = p.mm(ps_big[:, c0:c0 + 128],
                           src[:, ct * 128:(ct + 1) * 128], identb[:], True, True,
                           waits=(("act", evac_hist[reg]),), inc="pe")
                evac_hist[reg] = p.emit("scalar", lambda s, dst=dst, ct=ct, c0=c0:
                                        s.copy(dst[:, ct * 128:(ct + 1) * 128],
                                               ps_big[:, c0:c0 + 128]),
                                        waits=(("pe", tok),))
        if debug_taps:
            p.dma("sync", taps["tap_Psb"].ap(), P_sb[:], "dtap",
                  waits=(("act", p.counts["act"]),))
            p.dma("sync", taps["tap_Msb"].ap(), M_sb[:], "dtap",
                  waits=(("act", p.counts["act"]),))

        setup_end = dict(pe=p.counts["pe"], act=p.counts["act"], dve=p.counts["dve"])
        ck(1)
        prev_end = setup_end
        prev_dout = (0, 0)
        w1_use = [0, 0]
        w2_use = [0, 0]
        rl_use = [0, 0]
        out_use = [0, 0]
        gx_uses = [0, 0]
        sc_uses = [0, 0]

        ln_state = {"prev_ev": None, "xspill": None}

        def ln_tile(tt, extra_first=(), region="hi", trans_wait=None):
            sl = R[:, tt * C:(tt + 1) * C]
            p.emit("vector", lambda v, sl=sl: v.tensor_mul(sq_sb[:], sl, sl),
                   waits=extra_first)
            p.emit("vector", lambda v, sl=sl: v.reduce_sum(lnc[:, 0:1], sl, AX.X))
            p.emit("vector", lambda v: v.reduce_sum(lnc[:, 1:2], sq_sb[:], AX.X))
            p.emit("vector", lambda v: v.tensor_scalar_mul(lnc[:, 2:3], lnc[:, 0:1],
                                                           1.0 / C))
            p.emit("vector", lambda v: v.tensor_scalar(
                lnc[:, 3:4], lnc[:, 2:3], lnc[:, 2:3], -EPS, ALU.mult, ALU.add))
            d5 = p.emit("vector", lambda v: v.scalar_tensor_tensor(
                lnc[:, 4:5], lnc[:, 1:2], 1.0 / C, lnc[:, 3:4], ALU.mult,
                ALU.subtract))
            p.emit("scalar", lambda s: s.activation(lnc[:, 5:6], lnc[:, 4:5],
                                                    AF.Ln),
                   waits=(("dve", d5),))
            a2 = p.emit("scalar", lambda s: s.activation(lnc[:, 6:7], lnc[:, 5:6],
                                                         AF.Exp, scale=-0.5))
            p.emit("vector", lambda v: v.tensor_scalar_mul(
                lnc[:, 7:8], lnc[:, 2:3], -1.0))
            v3 = p.emit("vector", lambda v, sl=sl: v.tensor_scalar(
                xscr[:], sl, lnc[:, 7:8], lnc[:, 6:7], ALU.add, ALU.mult),
                waits=(("act", a2), ln_state.get("xspill")))
            if region == "hi":
                toks = []
                for ct in range(NCT):
                    w0 = ()
                    if ct == 0:
                        w0 = [("dve", v3)]
                        if ln_state["prev_ev"] is not None:
                            w0.append(("act", ln_state["prev_ev"]))
                    toks.append(p.mm(
                        ps_big[:, 1024 + ct * 128:1024 + (ct + 1) * 128],
                        xscr[:, ct * 128:(ct + 1) * 128], identb[:],
                        True, True, waits=w0,
                        inc="pe" if ct == NCT - 1 else None))
                ev = None
                for ct in range(NCT):
                    ev = p.emit("scalar", lambda s, ct=ct, tt=tt: s.copy(
                        xlnT[:, ct * T + tt * 128:ct * T + (tt + 1) * 128],
                        ps_big[:, 1024 + ct * 128:1024 + (ct + 1) * 128]),
                        waits=(("pe", toks[-1]),) if ct == 0 else ())
                ln_state["prev_ev"] = ev
                return ev
            p.dma("sync", bass.AP(xtok_d, tt * 128 * C, [[C, 128], [1, C]]),
                  xscr[:], "dxt", waits=(("dve", v3),))
            ln_state["xspill"] = ("dxt", p.counts["dxt"])
            ev_ring = [None, None]
            ev = None
            for ct in range(NCT):
                psr = ps_c if ct % 2 == 0 else ps_d
                w0 = [("dve", v3)] if ct == 0 else []
                if ct < 2:
                    w0.append(("dve", trans_wait))
                elif ev_ring[ct % 2] is not None:
                    w0.append(("act", ev_ring[ct % 2]))
                tm = p.mm(psr[:, 0:128],
                          xscr[:, ct * 128:(ct + 1) * 128], identb[:],
                          True, True, waits=w0, inc="pe")
                ev = p.emit("scalar", lambda s, ct=ct, tt=tt, psr=psr: s.copy(
                    xlnT[:, ct * T + tt * 128:ct * T + (tt + 1) * 128],
                    psr[:, 0:128]), waits=(("pe", tm),))
                ev_ring[ct % 2] = ev
            return ev

        for b in range(NB):
            bw = (("pe", prev_end["pe"]), ("act", prev_end["act"]),
                  ("dve", prev_end["dve"]), ("dout0", prev_dout[0]),
                  ("dout1", prev_dout[1]))
            p.dma("sync", R[:], bass.AP(x_d, b * T * C,
                                        [[C, 128], [128 * C, NT], [1, C]]),
                  "dx", waits=bw)
            DX = ("dx", p.counts["dx"])
            if b == 0:
                first_cast = True
                for wsrc, wdst in ((w1_d, w1bf_d), (w2_d, w2bf_d)):
                    for i in range(NE):
                        cw = (DC, DCG, DX) if first_cast else ()
                        first_cast = False
                        p.dma("gpsimd",
                              bass.AP(wdst, i * 8 * 128 * 4096,
                                      [[4096, 8 * 128], [1, 4096]]),
                              bass.AP(wsrc, i * 8 * 128 * 4096,
                                      [[4096, 8 * 128], [1, 4096]]),
                              "dcw", waits=cw)
                DCW = ("dcw", p.counts["dcw"])

            pdw = (("dout0", prev_dout[0]), ("dout1", prev_dout[1]))
            first_v = [DX, ("pe", prev_end["pe"]),
                       ("act", prev_end["act"])] + list(pdw)
            first_s = [DX, ("pe", prev_end["pe"]),
                       ("dve", prev_end["dve"])] + list(pdw)
            first_t = [DX, ("dve", prev_end["dve"]),
                       ("act", prev_end["act"])] + list(pdw)

            if b > 0:
                ln_state["prev_ev"] = prev_end["act"]
            ln_tok = None
            for tt in range(NT):
                extra = tuple(first_v) if tt == 0 else ()
                ln_tok = ln_tile(tt, extra_first=extra)

            if b == 0:
                ck(2)
            for ch in range(4):
                for ct in range(NCT):
                    w0 = []
                    if ch == 0 and ct == 0:
                        w0 = [("act", ln_tok)] + first_t
                    tok = p.mm(ps_big[:, ch * 512:(ch + 1) * 512],
                               wdkvT[:, ct * 128:(ct + 1) * 128],
                               xlnT[:, ct * T + ch * 512:ct * T + (ch + 1) * 512],
                               ct == 0, ct == NCT - 1, waits=w0,
                               inc="pe" if ct == NCT - 1 else None)
                p.emit("scalar", lambda s, ch=ch: s.copy(
                    ckvT[:, ch * 512:(ch + 1) * 512],
                    ps_big[:, ch * 512:(ch + 1) * 512]), waits=(("pe", tok),))
            ckvT_tok = p.counts["act"]
            for st in range(NT):
                for ct in range(NCT):
                    w0 = (("act", ckvT_tok),) if (st == 0 and ct == 0) else ()
                    tok = p.mm(ps_big[:, st * 128:(st + 1) * 128],
                               xlnT[:, ct * T + st * 128:ct * T + (st + 1) * 128],
                               wdkvT[:, ct * 128:(ct + 1) * 128],
                               ct == 0, ct == NCT - 1, waits=w0,
                               inc="pe" if ct == NCT - 1 else None)
            for ch in range(4):
                p.emit("scalar", lambda s, ch=ch: s.copy(
                    ckvN[:, ch * 512:(ch + 1) * 512],
                    ps_big[:, ch * 512:(ch + 1) * 512]),
                    waits=(("pe", tok),) if ch == 0 else ())
            ckvN_tok = p.counts["act"]
            for ch in range(4):
                for ct in range(NCT):
                    w0 = (("act", ckvN_tok),) if (ch == 0 and ct == 0) else ()
                    tok = p.mm(ps_big[:, ch * 512:(ch + 1) * 512],
                               wdqT[:, ct * 128:(ct + 1) * 128],
                               xlnT[:, ct * T + ch * 512:ct * T + (ch + 1) * 512],
                               ct == 0, ct == NCT - 1, waits=w0,
                               inc="pe" if ct == NCT - 1 else None)
                p.emit("scalar", lambda s, ch=ch: s.copy(
                    t1T[:, ch * 512:(ch + 1) * 512],
                    ps_big[:, ch * 512:(ch + 1) * 512]), waits=(("pe", tok),))
            t1_tok = p.counts["act"]
            if b == 0:
                ck(3)

            NSLOT = 4
            a_ring = [("act", t1_tok)] * NSLOT
            a_seq = [0]
            t_seq = [0]

            def slot_ap(slot, lo, hi):
                if slot < 4:
                    return ps_big[:, slot * 512 + lo:slot * 512 + hi]
                return (ps_a if slot == 4 else ps_b)[:, lo:hi]

            def emit_ql_group(qt, g):
                slot = a_seq[0] % NSLOT
                toks = []
                for k in range(4):
                    h = g * 4 + k
                    w0 = [a_ring[slot]] if k == 0 else []
                    toks.append(p.mm(
                        slot_ap(slot, k * 128, (k + 1) * 128),
                        P_sb[:, h * 128:(h + 1) * 128],
                        t1T[:, qt * 128:(qt + 1) * 128], True, True,
                        waits=w0, inc="pe" if k == 3 else None))
                ev = p.emit("vector", lambda v, g=g, slot=slot:
                            v.tensor_copy(
                                ql_sb[:, g * 512:(g + 1) * 512],
                                slot_ap(slot, 0, 512)),
                            waits=(("pe", toks[-1]),))
                a_ring[slot] = ("dve", ev)
                a_seq[0] += 1
                return ev

            def emit_tr_group(qt_p, h_p, g4, exps_p, guard_p):
                slot = a_seq[0] % NSLOT
                st0 = g4 * 4
                nst = min(4, qt_p + 1 - st0)
                tm = None
                for k in range(nst):
                    st = st0 + k
                    w0 = []
                    if k == 0:
                        w0 = [a_ring[slot],
                              ("act", exps_p[min(g4, len(exps_p) - 1)])]
                    tm = p.mm(slot_ap(slot, k * 128, (k + 1) * 128),
                              ptk[:, st * 128:(st + 1) * 128], identb[:],
                              True, True, waits=w0,
                              inc="pe" if k == nst - 1 else None)
                eng = "vector"
                ew = [("pe", tm)]
                if guard_p is not None:
                    ew.append(guard_p)
                dap = bass.AP(p_ar, st0 * T + h_p * 128,
                              [[NT * T, 128], [T, nst], [1, 128]])
                if eng == "vector":
                    evt = p.emit(eng, lambda v, dap=dap, slot=slot,
                                 nst=nst: v.tensor_copy(
                                     dap, slot_ap(slot, 0, nst * 128)),
                                 waits=ew)
                    a_ring[slot] = ("dve", evt)
                else:
                    evt = p.emit(eng, lambda s, dap=dap, slot=slot,
                                 nst=nst: s.copy(
                                     dap, slot_ap(slot, 0, nst * 128)),
                                 waits=ew)
                    a_ring[slot] = ("act", evt)
                t_seq[0] += 1
                a_seq[0] += 1
                return tm

            pend = [None]
            rs_tok = {0: None, 1: None}

            def flush_ptr():
                if pend[0] is None:
                    return
                qt_p, h_p, exps_p, guard_p = pend[0]
                for g4 in range((qt_p + 4) // 4):
                    emit_tr_group(qt_p, h_p, g4, exps_p, guard_p)
                pend[0] = None

            def emit_head_step(qt, h, qlw, ctx_guard):
                span_ = (qt + 1) * 128
                nch_ = (span_ + 511) // 512
                prev = pend[0]
                ngr_p = 0 if prev is None else (prev[0] + 4) // 4
                exps = []
                for j in range(max(nch_, ngr_p)):
                    mmt = mtok = None
                    if j < nch_:
                        slot = a_seq[0] % NSLOT
                        n = min(512, span_ - j * 512)
                        w0 = [a_ring[slot]]
                        if j == 0:
                            w0.append(("dve", qlw))
                        mmt = p.mm(slot_ap(slot, 0, n),
                                   ql_sb[:, h * 128:(h + 1) * 128],
                                   ckvT[:, j * 512:j * 512 + n], True, True,
                                   waits=w0, inc="pe")
                        if j == nch_ - 1:
                            off = qt * 128 - j * 512
                            mtok = p.emit(
                                "vector", lambda v, slot=slot, off=off:
                                v.tensor_tensor(
                                    slot_ap(slot, off, off + 128),
                                    slot_ap(slot, off, off + 128),
                                    mask2[:], ALU.add), waits=(("pe", mmt),))
                        a_seq[0] += 1
                    trt = None
                    if prev is not None and j < ngr_p:
                        trt = emit_tr_group(prev[0], prev[1], j, prev[2],
                                            prev[3])
                    if j < nch_:
                        ew = [("pe", trt if trt is not None else mmt)]
                        if mtok is not None:
                            ew.append(("dve", mtok))
                        if bexp0[0]:
                            ew.append(("dve", bcol_tok))
                            bexp0[0] = False
                        sco = (h % 2) * 4
                        if j == 0 and rs_tok[h % 2] is not None:
                            ew.append(("dve", rs_tok[h % 2]))
                        et = p.emit(
                            "scalar",
                            lambda s, j=j, slot=slot, n=n, sco=sco:
                            s.activation(
                                ptk[:, j * 512:j * 512 + n],
                                slot_ap(slot, 0, n),
                                AF.Exp, bias=bcol[:, 0:1],
                                accum_out=sch[:, sco + j:sco + j + 1]),
                            waits=ew)
                        a_ring[slot] = ("act", et)
                        exps.append(et)
                sc = (qt % 2) * NH
                sco = (h % 2) * 4
                rs_tok[h % 2] = p.emit(
                    "vector", lambda v, h=h, nch_=nch_, sc=sc, sco=sco:
                    v.reduce_sum(
                        sumsf[:, sc + h:sc + h + 1],
                        sch[:, sco:sco + nch_], AX.X),
                    waits=(("act", exps[-1]),))
                pend[0] = (qt, h, exps, ctx_guard)

            for g in range(4):
                qlw = emit_ql_group(0, g)
                for h in range(4 * g, 4 * g + 4):
                    emit_head_step(0, h, qlw, None)

            gate_tok_prev = None
            yev_last = None
            ln2_ev = None
            for qt in range(NT):
                flush_ptr()
                chain_act = p.counts["act"]
                chain_dve = p.counts["dve"]
                ctx_ev = {}
                ctx_stop = None
                for c in range(4):
                    cc = c * 512
                    for si in range(qt + 1):
                        w0 = []
                        if si == 0:
                            w0 = [("act", chain_act), ("dve", chain_dve)]
                            if c == 0:
                                if ln2_ev is not None:
                                    w0.append(("act", ln2_ev))
                                if yev_last is not None:
                                    w0.append(("dve", yev_last))
                            else:
                                w0.append(ctx_ev[c - 1])
                        ctx_stop = p.mm(ps_c,
                                        ckvN[:, si * 128:(si + 1) * 128],
                                        p_ar[:, si * T + cc:si * T + cc + 512],
                                        si == 0, si == qt, waits=w0,
                                        inc="pe" if si == qt else None)
                    eng = "vector" if (c % 2) else "scalar"
                    if eng == "vector":
                        cev = p.emit(eng, lambda v, cc=cc: v.tensor_copy(
                            ctx_sb[:, cc:cc + 512], ps_c[:]),
                            waits=(("pe", ctx_stop),))
                        ctx_ev[c] = ("dve", cev)
                    else:
                        cev = p.emit(eng, lambda s, cc=cc: s.copy(
                            ctx_sb[:, cc:cc + 512], ps_c[:]),
                            waits=(("pe", ctx_stop),))
                        ctx_ev[c] = ("act", cev)
                    if qt + 1 < NT:
                        qlw = emit_ql_group(qt + 1, c)
                        for h in range(4 * c, 4 * c + 4):
                            emit_head_step(qt + 1, h, qlw,
                                            ("pe", ctx_stop))
                sc = (qt % 2) * NH
                rc = p.emit("vector", lambda v, sc=sc: v.reciprocal(
                    rinv16[:, 0:NH], sumsf[:, sc:sc + NH]),
                    waits=(("act", chain_act),))
                ytoks = []
                for h in range(NH):
                    psy = ps_c if h < 8 else ps_d
                    col = (h % 8) * 64
                    w0 = []
                    if h == 0:
                        w0 = [ctx_ev[0], ctx_ev[1], ctx_ev[2], ctx_ev[3]]
                    if h == 8 and gate_tok_prev is not None:
                        w0 = [("dve", gate_tok_prev)]
                    ytoks.append(p.mm(
                        psy[:, col:col + 64],
                        ctx_sb[:, h * 128:(h + 1) * 128],
                        M_sb[:, h * 64:(h + 1) * 64],
                        h == 0 or h == 8, h == 7 or h == 15,
                        waits=w0, inc="pe" if h == NH - 1 else None))
                yev = None
                for h in range(NH):
                    psy = ps_c if h < 8 else ps_d
                    col = (h % 8) * 64
                    w0 = [("pe", ytoks[-1])] if h == 0 else []
                    yev = p.emit("vector", lambda v, qt=qt, h=h, psy=psy,
                                 col=col: v.scalar_tensor_tensor(
                                     R[:, qt * C + h * 64:qt * C + (h + 1) * 64],
                                     psy[:, col:col + 64], rinv16[:, h:h + 1],
                                     R[:, qt * C + h * 64:qt * C + (h + 1) * 64],
                                     ALU.mult, ALU.add), waits=w0)
                yev_last = yev
                ln2_ev = ln_tile(qt, region="cd", trans_wait=yev)
                for ct in range(NCT):
                    w0 = []
                    if ct == 0:
                        w0 = [("act", ln2_ev), ("dve", yev)]
                    lg = p.mm(ps_d[:, 0:8], xlnT[:, ct * T + qt * 128:
                                                 ct * T + (qt + 1) * 128],
                              wr[:, ct * NE:(ct + 1) * NE], ct == 0,
                              ct == NCT - 1,
                              waits=w0, inc="pe" if ct == NCT - 1 else None)
                p.emit("vector", lambda v: v.reduce_max(gc[:, 0:1],
                                                        ps_d[:, 0:8], AX.X),
                       waits=(("pe", lg),))
                p.emit("vector", lambda v: v.tensor_scalar(
                    gmask[:], ps_d[:, 0:8], gc[:, 0:1], None, ALU.is_equal))
                p.emit("vector", lambda v: v.scalar_tensor_tensor(
                    gm[:], gmask[:], -3.0e38, ps_d[:, 0:8], ALU.mult, ALU.add))
                p.emit("vector", lambda v: v.tensor_tensor_scan(
                    gsc[:], gmask[:], gmask[:], 0.0, ALU.add, ALU.bypass))
                p.emit("vector", lambda v: v.tensor_scalar(
                    gsc[:], gsc[:], 1.0, None, ALU.is_equal))
                p.emit("vector", lambda v: v.tensor_mul(gmask[:], gmask[:],
                                                        gsc[:]))
                p.emit("vector", lambda v: v.reduce_max(gc[:, 1:2], gm[:],
                                                        AX.X))
                gd = p.emit("vector", lambda v: v.tensor_sub(gc[:, 2:3],
                                                             gc[:, 0:1],
                                                             gc[:, 1:2]))
                ga = p.emit("scalar", lambda s: s.activation(gc[:, 3:4],
                                                             gc[:, 2:3],
                                                             AF.Exp,
                                                             scale=-1.0),
                            waits=(("dve", gd),))
                p.emit("vector", lambda v: v.tensor_scalar_add(gc[:, 4:5],
                                                               gc[:, 3:4],
                                                               1.0),
                       waits=(("act", ga),))
                p.emit("vector", lambda v: v.reciprocal(gc[:, 5:6],
                                                        gc[:, 4:5]))
                gate_tok_prev = p.emit("vector", lambda v, qt=qt:
                                       v.tensor_scalar(
                                           gate[:, qt * NE:(qt + 1) * NE],
                                           gmask[:], gc[:, 5:6], None,
                                           ALU.mult))
                p.emit("vector", lambda v, qt=qt: v.reduce_sum(
                    wcol[:, qt:qt + 1], gate[:, qt * NE:(qt + 1) * NE], AX.X))
                if b == 0 and qt == 0:
                    ck(4)

            if b == 0:
                ck(5)
            att_end = dict(pe=p.counts["pe"], act=p.counts["act"],
                           dve=p.counts["dve"])
            last_add = att_end["dve"]
            p.dma("sync", rspill_d.ap(), R[:], "dsp", waits=(("dve", last_add),))
            DSP = ("dsp", p.counts["dsp"])
            if b == 0:
                ck(6)

            if debug_taps and b == 1:
                aw = (("dve", att_end["dve"]), ("act", att_end["act"]))
                p.dma("sync", taps["tap_xmid"].ap(), R[:], "dtap", waits=aw)
                p.dma("sync", taps["tap_gate"].ap(), gate[:], "dtap", waits=aw)
                p.dma("sync", taps["tap_ckvT"].ap(), ckvT[:], "dtap", waits=aw)
                p.dma("sync", taps["tap_ckvN"].ap(), ckvN[:], "dtap", waits=aw)
                p.dma("sync", taps["tap_t1T"].ap(), t1T[:], "dtap", waits=aw)
                p.dma("sync", taps["tap_xlnT"].ap(), xlnT[:], "dtap", waits=aw)
                p.dma("sync", taps["tap_p"].ap(), p_ar[:], "dtap", waits=aw)

            ATTW = [("pe", att_end["pe"]), ("dve", att_end["dve"]),
                    ("act", att_end["act"])]
            if debug_taps:
                ATTW.append(("dtap", p.counts["dtap"]))

            CAP = 512
            BIGI = 16384.0
            ATTW = [("pe", att_end["pe"]), ("dve", att_end["dve"]),
                    ("act", att_end["act"])]
            if debug_taps:
                ATTW.append(("dtap", p.counts["dtap"]))

            gtoks = []
            for tt in range(NT):
                w0 = list(ATTW) if tt == 0 else []
                gtoks.append(p.mm(
                    ps_big[0:8, tt * 128:(tt + 1) * 128],
                    gate[:, tt * NE:(tt + 1) * NE], identf[:], True, True,
                    waits=w0, inc="pe" if tt == NT - 1 else None))
            gev = None
            for chk in range(4):
                gev = p.emit("scalar", lambda s, chk=chk: s.copy(
                    gateT_sb[:, chk * 512:(chk + 1) * 512],
                    ps_big[0:8, chk * 512:(chk + 1) * 512]),
                    waits=(("pe", gtoks[-1]),) if chk == 0 else ())
            p.emit("vector", lambda v: v.tensor_scalar(
                maskr[:], gateT_sb[:], 0.0, None, ALU.is_gt),
                waits=(("act", gev),))
            p.emit("vector", lambda v: v.tensor_tensor_scan(
                scanr[:], maskr[:], maskr[:], 0.0, ALU.add, ALU.bypass))
            p.emit("vector", lambda v: v.tensor_sub(scanr[:], scanr[:],
                                                    maskr[:]))
            p.emit("vector", lambda v: v.tensor_scalar(
                rowa[:], scanr[:], 128.0, None, ALU.is_ge))
            p.emit("vector", lambda v: v.tensor_scalar(
                rowb[:], scanr[:], 256.0, None, ALU.is_ge))
            p.emit("vector", lambda v: v.tensor_add(rowa[:], rowa[:], rowb[:]))
            p.emit("vector", lambda v: v.tensor_scalar(
                rowb[:], scanr[:], 384.0, None, ALU.is_ge))
            p.emit("vector", lambda v: v.tensor_add(rowa[:], rowa[:], rowb[:]))
            p.emit("vector", lambda v: v.scalar_tensor_tensor(
                rowb[:], rowa[:], -128.0, scanr[:], ALU.mult, ALU.add))
            p.emit("vector", lambda v: v.tensor_scalar(
                rowa[:], rowa[:], ecol[0:8, 0:1], None, ALU.add))
            p.emit("vector", lambda v: v.scalar_tensor_tensor(
                rowb[:], rowb[:], 32.0, rowa[:], ALU.mult, ALU.add))
            p.emit("vector", lambda v: v.tensor_scalar(
                rowa[:], scanr[:], float(CAP), None, ALU.is_ge))
            p.emit("vector", lambda v: v.scalar_tensor_tensor(
                rowb[:], rowa[:], BIGI, rowb[:], ALU.mult, ALU.add))
            msk_tok = p.emit("vector", lambda v: v.tensor_mul(
                rowb[:], rowb[:], maskr[:]))
            cmb = []
            for chk in range(4):
                cmb.append(p.mm(ps_big[0:1, chk * 512:(chk + 1) * 512],
                                ones8[0:8, 0:1],
                                rowb[:, chk * 512:(chk + 1) * 512], True, True,
                                waits=(("dve", msk_tok),) if chk == 0 else (),
                                inc="pe" if chk == 3 else None))
            prow_ev = p.emit("scalar", lambda s: s.copy(posrow[0:1, :],
                                                        ps_big[0:1, 0:2048]),
                             waits=(("pe", cmb[-1]),))
            ptoks = []
            for tt in range(NT):
                ptoks.append(p.mm(
                    ps_big[:, 1024 + tt:1024 + tt + 1],
                    posrow[0:1, tt * 128:(tt + 1) * 128], ones1f[0:1, 0:1],
                    True, True, waits=(("act", prow_ev),) if tt == 0 else (),
                    inc="pe" if tt == NT - 1 else None))
            pf_ev = p.emit("scalar", lambda s: s.copy(
                posf[:], ps_big[:, 1024:1024 + 16]),
                waits=(("pe", ptoks[-1]),))
            pi_tok = p.emit("vector", lambda v: v.tensor_copy(posi[:], posf[:]),
                            waits=(("act", pf_ev),))
            p.emit("vector", lambda v: v.memset(idxini[:], 16384))
            ini_tok = p.counts["dve"]
            p.dma("gpsimd", idxl_d.ap(), idxini[:], "didx",
                  waits=(("dve", ini_tok),) + tuple(ATTW))
            ini_done = p.counts["didx"]
            for tt in range(NT):
                def fn(g, tt=tt):
                    return g.indirect_dma_start(
                        bass.AP(idxl_d, 0, [[1, 4096], [1, 1]]),
                        bass.IndirectOffsetOnAxis(posi[:, tt:tt + 1], 0),
                        iota_sb[:, tt:tt + 1], None,
                        bounds_check=REGS["b4095"], oob_is_err=False)
                p.emit("gpsimd", fn,
                       waits=(("dve", pi_tok), ("didx", ini_done))
                       if tt == 0 else (),
                       inc="didx", n=16)
            scat_done = p.counts["didx"]
            p.dma("gpsimd", idx_sb[:], idxl_d.ap(), "didx",
                  waits=(("didx", scat_done),))
            idx_done = p.counts["didx"]
            z_tok = p.emit("vector", lambda v: v.memset(yld[:, 0:1024], 0))
            for tt in range(NT):
                p.dma("sync", bass.AP(ytok_d, tt * 128 * C, [[C, 128], [1, C]]),
                      yld[:, 0:1024], "dz0",
                      waits=(("dve", z_tok),) if tt == 0 else ())
            dz_done = p.counts["dz0"]
            if b == 0:
                ck(6.2)

            w1_free = [None, None]
            w2_free = [None, None]
            gx_free = [None, None]
            gxT_free = [None, None]
            gxtr_ring = [None, None, None, None]
            gxtr_seq = [0]
            ytr_free = None
            ytr_free2 = [(), ()]
            ybf_free = None
            yevh = {}
            gelu_hist = {}
            h_seq = 0
            y_seq = 0
            ytt_use = [0, 0]
            sc_count = 0
            for e in range(NE):
                DZW = ("dz0", dz_done) if e == 0 else None
                gxw = gxT_free[e % 2]
                gx_ev = None
                for g4 in range(4):
                    slot = g4 % 2
                    mz = p.emit("vector", lambda v, slot=slot: v.memset(
                        gx_tok[:, slot * 1024:(slot + 1) * 1024], 0),
                        waits=(("pe", gx_free[slot]),)
                        if gx_free[slot] is not None else (DSP,))
                    dw = [("didx", idx_done), ("dxt", p.counts["dxt"]),
                          ("dve", mz)]
                    def gfn(gp, e=e, g4=g4, slot=slot):
                        return gp.indirect_dma_start(
                            gx_tok[:, slot * 1024:(slot + 1) * 1024], None,
                            bass.AP(xtok_d, 0, [[C, T], [1, C]]),
                            bass.IndirectOffsetOnAxis(
                                idx_sb[:, e * 4 + g4:e * 4 + g4 + 1], 0),
                            bounds_check=REGS["b2047"], oob_is_err=False)
                    gx_uses[slot] += 1
                    p.emit("gpsimd", gfn, waits=dw, inc="dgx%d" % slot, n=16)
                    guse = ("dgx%d" % slot, 16 * gx_uses[slot])
                    tok = None
                    for half in range(2):
                        gb = gxtr_seq[0] % 4
                        for k in range(4):
                            ct = half * 4 + k
                            w0 = []
                            if k == 0:
                                if gxtr_ring[gb] is not None:
                                    w0.append(gxtr_ring[gb])
                                if half == 0:
                                    w0.append(guse)
                                    if gxw is not None:
                                        w0.append(gxw)
                                        gxw = None
                                    if g4 == 0 and ytr_free is not None:
                                        w0.append(ytr_free)
                            tok = p.mm(
                                ps_big[:, gb * 512 + k * 128:
                                       gb * 512 + (k + 1) * 128],
                                gx_tok[:, slot * 1024 + ct * 128:
                                       slot * 1024 + (ct + 1) * 128],
                                identb[:], True, True, waits=w0,
                                inc="pe" if k == 3 else None)
                        dap = bass.AP(gxT, (e % 2) * 4096 +
                                      half * 4 * 512 + g4 * 128,
                                      [[2 * NCT * 512, 128], [512, 4],
                                       [1, 128]])
                        eng = "vector" if gxtr_seq[0] % 2 else "scalar"
                        if eng == "vector":
                            ev = p.emit(eng, lambda v, dap=dap, gb=gb:
                                        v.tensor_copy(
                                            dap,
                                            ps_big[:, gb * 512:
                                                   gb * 512 + 512]),
                                        waits=(("pe", tok),))
                            gxtr_ring[gb] = ("dve", ev)
                        else:
                            ev = p.emit(eng, lambda s, dap=dap, gb=gb:
                                        s.copy(
                                            dap,
                                            ps_big[:, gb * 512:
                                                   gb * 512 + 512]),
                                        waits=(("pe", tok),))
                            gxtr_ring[gb] = ("act", ev)
                        gxtr_seq[0] += 1
                    gx_free[slot] = tok
                gx_ev = (p.counts["act"], p.counts["dve"])
                if b == 0 and e == 0:
                    ck(6.5)

                for htg in range(8):
                    slot = htg % 2
                    w1_use[slot] += 1
                    dw = []
                    if w1_free[slot] is not None:
                        dw.append(("pe", w1_free[slot]))
                    if b == 0 and e == 0 and htg < 2:
                        dw += ATTW
                    if b == 0 and e == 0:
                        dw.append(DCW)
                    p.dma("sync", w1ring[:, slot * 4096:(slot + 1) * 4096],
                          bass.AP(w1bf_d, (e * 8 + htg) * 128 * 4096,
                                  [[4096, 128], [1, 4096]]),
                          "w1%d" % slot, waits=dw)
                    for htl in range(4):
                        ht = htg * 4 + htl
                        reg = h_seq % 2
                        psh = ps_a if reg == 0 else ps_b
                        toks = []
                        for ct in range(NCT):
                            w0 = []
                            if ct == 0:
                                if htl == 0:
                                    w0.append(("w1%d" % slot,
                                               16 * w1_use[slot]))
                                if ht == 0:
                                    w0 += [("act", gx_ev[0]),
                                           ("dve", gx_ev[1])]
                                prev = gelu_hist.get(h_seq - 2)
                                if prev is not None:
                                    w0.append(("act", prev))
                            toks.append(p.mm(
                                psh,
                                w1ring[:, slot * 4096 + ct * 512 +
                                       htl * 128:slot * 4096 + ct * 512 +
                                       (htl + 1) * 128],
                                gxT[:, (e % 2) * 4096 + ct * 512:
                                    (e % 2) * 4096 + (ct + 1) * 512],
                                ct == 0, ct == NCT - 1,
                                waits=w0, inc="pe" if ct == NCT - 1 else None))
                        gw2 = [("pe", toks[-1])]
                        if b == 0 and e == 0 and ht == 0:
                            gw2.append(DSP)
                        gelu_hist[h_seq] = p.emit(
                            "scalar", lambda s, ht=ht, psh=psh: s.activation(
                                h_g[:, ht * 512:(ht + 1) * 512], psh[:],
                                AF.Gelu), waits=gw2)
                        h_seq += 1
                        if htl == 3:
                            w1_free[slot] = toks[-1]

                ybf_ev = None
                for ct in range(NCT):
                    slot = ct % 2
                    w2_use[slot] += 1
                    dw = []
                    if w2_free[slot] is not None:
                        dw.append(("pe", w2_free[slot]))
                    if b == 0 and e == 0 and ct < 2:
                        dw += ATTW
                    if b == 0 and e == 0:
                        dw.append(DCW)
                    p.dma("sync", w2ring[:, slot * 4096:(slot + 1) * 4096],
                          bass.AP(w2bf_d, (e * 8 + ct) * 128 * 4096,
                                  [[4096, 128], [1, 4096]]),
                          "w2%d" % slot, waits=dw)
                    reg = y_seq % 2
                    psy = ps_c if reg == 0 else ps_d
                    toks = []
                    for ht in range(NHT):
                        w0 = []
                        if ht == 0:
                            w0.append(("w2%d" % slot, 16 * w2_use[slot]))
                            w0.append(("act", gelu_hist[h_seq - 1]))
                            prev = yevh.get(y_seq - 2)
                            if prev is not None:
                                w0.append(prev)
                            if ct == 0 and ybf_free is not None:
                                w0.append(("pe", ybf_free))
                        toks.append(p.mm(
                            psy,
                            w2ring[:, slot * 4096 + ht * 128:
                                   slot * 4096 + (ht + 1) * 128],
                            h_g[:, ht * 512:(ht + 1) * 512],
                            ht == 0, ht == NHT - 1,
                            waits=w0, inc="pe" if ht == NHT - 1 else None))
                    eng = "vector" if ct % 2 else "scalar"
                    if eng == "vector":
                        ev = p.emit(eng, lambda v, ct=ct, psy=psy:
                                    v.tensor_copy(
                                        ybf[:, ct * 512:(ct + 1) * 512],
                                        psy[:]), waits=(("pe", toks[-1]),))
                        yevh[y_seq] = ("dve", ev)
                    else:
                        ev = p.emit(eng, lambda s, ct=ct, psy=psy: s.copy(
                            ybf[:, ct * 512:(ct + 1) * 512], psy[:]),
                            waits=(("pe", toks[-1]),))
                        yevh[y_seq] = ("act", ev)
                    ybf_ev = yevh[y_seq]
                    y_seq += 1
                    w2_free[slot] = toks[-1]

                ybf_ev2 = yevh.get(y_seq - 2)
                ytr_last = None
                for g4 in range(4):
                    odd = g4 % 2
                    toks = []
                    for ct in range(NCT):
                        w0 = []
                        if ct == 0:
                            w0.append(ybf_ev)
                            if g4 == 0 and ybf_ev2 is not None:
                                w0.append(ybf_ev2)
                            for t_ in ytr_free2[odd]:
                                w0.append(t_)
                        if odd:
                            dst = (ps_a if ct < 4 else ps_b)[
                                :, (ct % 4) * 128:(ct % 4 + 1) * 128]
                        else:
                            dst = ps_big[:, 1024 + ct * 128:
                                         1024 + (ct + 1) * 128]
                        toks.append(p.mm(
                            dst,
                            ybf[:, ct * 512 + g4 * 128:ct * 512 +
                                g4 * 128 + 128],
                            identb[:], True, True,
                            waits=w0, inc="pe" if ct == NCT - 1 else None))
                    slot = (e * 4 + g4) % 2
                    ytt_use[slot] += 1
                    ew = [("pe", toks[-1])]
                    if sc_uses[slot] > 0:
                        ew.append(("dsc%d" % slot, 16 * sc_uses[slot]))
                    if odd:
                        yt_a = p.emit("scalar", lambda s, slot=slot: s.copy(
                            ytt[:, slot * 1024:slot * 1024 + 512],
                            ps_a[:]), waits=ew)
                        yt_d = p.emit("vector", lambda v, slot=slot:
                                      v.tensor_copy(
                                          ytt[:, slot * 1024 + 512:
                                              (slot + 1) * 1024],
                                          ps_b[:]), waits=ew)
                        ytr_free2[1] = (("act", yt_a), ("dve", yt_d))
                        scw = (("act", yt_a), ("dve", yt_d), DZW)
                    else:
                        yt_ev = p.emit("scalar", lambda s, slot=slot: s.copy(
                            ytt[:, slot * 1024:(slot + 1) * 1024],
                            ps_big[:, 1024:2048]), waits=ew)
                        ytr_free2[0] = (("act", yt_ev),)
                        ytr_free = ("act", yt_ev)
                        scw = (("act", yt_ev), DZW)
                    ytr_last = toks[-1]
                    def sfn(gp, e=e, g4=g4, slot=slot):
                        return gp.indirect_dma_start(
                            bass.AP(ytok_d, 0, [[C, T], [1, C]]),
                            bass.IndirectOffsetOnAxis(
                                idx_sb[:, e * 4 + g4:e * 4 + g4 + 1], 0),
                            ytt[:, slot * 1024:(slot + 1) * 1024], None,
                            bounds_check=REGS["b2047"], oob_is_err=False)
                    sc_uses[slot] += 1
                    p.emit("gpsimd", sfn, waits=scw,
                           inc="dsc%d" % slot, n=16)
                    sc_count += 1
                ybf_free = ytr_last
                gxT_free[e % 2] = ("pe", toks[-1])
                if debug_taps and b == 1 and e == 0:
                    tw = (("act", p.counts["act"]), ("dve", p.counts["dve"]),
                          ("didx", p.counts["didx"]))
                    p.dma("sync", taps["tap_idx"].ap(), idx_sb[:], "dtap",
                          waits=tw)
                    p.dma("sync", taps["tap_gxT"].ap(),
                          gxT[:, 0:NCT * 512], "dtap", waits=tw)
                    p.dma("sync", taps["tap_ybf"].ap(), ybf[:], "dtap",
                          waits=tw)
                    p.dma("sync", taps["tap_hg"].ap(), h_g[:], "dtap",
                          waits=tw)
                if b == 0 and e == 0:
                    ck(7)

            if b == 0:
                ck(7.5)
            rl_free = [None, None]
            yld_free = [None, None]
            out_free = [None, None]
            rl_use2 = [0, 0]
            yld_use = [0, 0]
            sc_all0 = p.counts["dsc0"]
            sc_all1 = p.counts["dsc1"]
            for tt in range(NT):
                slot = tt % 2
                rl_use[slot] += 1
                rl_use2[slot] += 1
                dw = [DSP]
                if rl_free[slot] is not None:
                    dw.append(("dve", rl_free[slot]))
                if tt < 2:
                    dw += ATTW
                p.dma("sync", rl_buf[:, slot * 1024:(slot + 1) * 1024],
                      bass.AP(rspill_d, tt * 1024, [[NT * C, 128], [1, 1024]]),
                      "drl%d" % slot, waits=dw)
                yld_use[slot] += 1
                dw2 = [("dsc0", sc_all0), ("dsc1", sc_all1)]
                if yld_free[slot] is not None:
                    dw2.append(("dve", yld_free[slot]))
                if fvar == 2:
                    dw2 = [w for w in dw2 if w[0] != "dsc"]
                if fvar != 3:
                    p.dma("sync", yld[:, slot * 1024:(slot + 1) * 1024],
                          bass.AP(ytok_d, tt * 128 * C, [[C, 128], [1, C]]),
                          "dyl", waits=dw2)
                yl_tok = p.counts["dyl"]
                ow = [("drl%d" % slot, 16 * rl_use[slot])]
                if fvar != 3:
                    ow.append(("dyl", yl_tok))
                if out_free[slot] is not None:
                    ow.append(("dout%d" % slot, out_free[slot]))
                if fvar in (1, 3):
                    at_ = p.emit("vector", lambda v, slot=slot:
                                 v.tensor_copy(
                                     out_st[:, slot * 1024:(slot + 1) * 1024],
                                     rl_buf[:, slot * 1024:(slot + 1) * 1024]),
                                 waits=ow)
                else:
                    at_ = p.emit("vector", lambda v, slot=slot, tt=tt:
                                 v.scalar_tensor_tensor(
                                     out_st[:, slot * 1024:(slot + 1) * 1024],
                                     yld[:, slot * 1024:(slot + 1) * 1024],
                                     wcol[:, tt:tt + 1],
                                     rl_buf[:, slot * 1024:(slot + 1) * 1024],
                                     ALU.mult, ALU.add), waits=ow)
                rl_free[slot] = at_
                yld_free[slot] = at_
                p.dma("sync", bass.AP(out_d, b * T * C + tt * 128 * C,
                                      [[C, 128], [1, C]]),
                      out_st[:, slot * 1024:(slot + 1) * 1024],
                      "dout%d" % slot, waits=(("dve", at_),))
                out_free[slot] = p.counts["dout%d" % slot]

            prev_end = dict(pe=p.counts["pe"], act=p.counts["act"],
                            dve=p.counts["dve"])
            prev_dout = (p.counts["dout0"], p.counts["dout1"])
            if b == 0:
                ck(8)

    except _Stop:
        pass

    fw = [(sname, p.counts[sname]) for sname in p.semnames]
    p.emit("sync", lambda e: None, waits=fw)

    semh = {}
    for name in p.semnames:
        semh[name] = es.enter_context(nc.semaphore(name))

    with nc.Block() as block:
        def runner(engname):
            def run_ops(eng):
                for fn, waits, inc, n in p.ops[engname]:
                    for s, v in waits:
                        eng.wait_ge(semh[s], v)
                    inst = fn(eng)
                    if inc is not None:
                        assert inst is not None, engname
                        inst.then_inc(semh[inc], n)

            def body(eng):
                if engname == "gpsimd":
                    with eng.register("rb2047") as r1, \
                            eng.register("rb4095") as r2:
                        eng.reg_mov(r1, 2047)
                        eng.reg_mov(r2, 4095)
                        REGS["b2047"] = r1
                        REGS["b4095"] = r2
                        run_ops(eng)
                else:
                    run_ops(eng)
            return body
        block.tensor(runner("tensor"))
        block.vector(runner("vector"))
        block.scalar(runner("scalar"))
        block.sync(runner("sync"))
        block.gpsimd(runner("gpsimd"))

    es.close()
    return nc, taps


_BF = ml_dtypes.bfloat16


def _consts():
    ii = np.arange(128)
    return dict(
        identf=np.eye(128, dtype=np.float32),
        identb=np.eye(128, dtype=_BF),
        ones1f=np.ones((1, 128), dtype=np.float32),
        ones1b=np.ones((1, 128), dtype=_BF),
        onescol=np.ones((128, 1), dtype=_BF),
        mask2=np.where(ii[None, :] <= ii[:, None], 0.0, NEGC).astype(np.float32),
        ones8=np.ones((8, 1), dtype=np.float32),
        ecol=(4.0 * np.arange(8, dtype=np.float32))[:, None].copy(),
        iota_tok=(np.arange(16, dtype=np.int32)[None, :] * 128
                  + np.arange(128, dtype=np.int32)[:, None]).copy(),
    )


def kernel(L0_attn_x, L0_attn_W_dq, L0_attn_W_uq, L0_attn_W_dkv, L0_attn_W_uk,
           L0_attn_W_uv, L0_attn_W_o, L0_ffn_Wr, L0_ffn_W1, L0_ffn_W2):
    x = np.ascontiguousarray(np.asarray(L0_attn_x, dtype=np.float32))
    weights = dict(
        wdq=np.ascontiguousarray(L0_attn_W_dq, dtype=np.float32),
        wuq=np.ascontiguousarray(L0_attn_W_uq, dtype=np.float32),
        wdkv=np.ascontiguousarray(L0_attn_W_dkv, dtype=np.float32),
        wuk=np.ascontiguousarray(L0_attn_W_uk, dtype=np.float32),
        wuv=np.ascontiguousarray(L0_attn_W_uv, dtype=np.float32),
        wo=np.ascontiguousarray(L0_attn_W_o, dtype=np.float32),
        wr=np.ascontiguousarray(L0_ffn_Wr, dtype=np.float32),
        w1=np.ascontiguousarray(
            np.asarray(L0_ffn_W1, dtype=np.float32)
            .reshape(NE, 8, 128, 8, 512).transpose(0, 3, 2, 1, 4)
            .reshape(NE * 8, 128, 4096)),
        w2=np.ascontiguousarray(
            np.asarray(L0_ffn_W2, dtype=np.float32)
            .reshape(NE, 32, 128, 8, 128).transpose(0, 3, 2, 1, 4)
            .reshape(NE * 8, 128, 4096)),
    )
    weights.update(_consts())
    nc, _ = build(debug_taps=False)
    in_maps = []
    for i in range(NCORES):
        m = dict(weights)
        m["x"] = x[i * NB:(i + 1) * NB]
        in_maps.append(m)
    res = run_bass_kernel_spmd(nc, in_maps, core_ids=list(range(NCORES)))
    out = np.concatenate([r["out"] for r in res.results], axis=0)
    return out
